# version 04
# baseline (speedup 1.0000x reference)
from collections import defaultdict

COMPUTE = ("tensor", "vector", "scalar", "gpsimd")
ENGINES = ("tensor", "vector", "scalar", "gpsimd", "sync")
QUEUES = {"qsync": "sync", "qgp": "gpsimd", "qscal": "scalar"}
ALL = list(ENGINES) + list(QUEUES)


_BUF_SERIAL = [0]


class Buf:
    __slots__ = ("name", "writer", "readers", "serial")

    def __init__(self, name):
        self.name = name
        self.writer = None
        self.readers = []
        _BUF_SERIAL[0] += 1
        self.serial = _BUF_SERIAL[0]


class FW:
    def __init__(self, nc, tag=""):
        self.nc = nc
        self.tag = tag
        self.sems = {}
        self.count = {s: 0 for s in ALL}
        self.step = {s: (16 if s in QUEUES else 1) for s in ALL}
        self.ops = {s: [] for s in ALL}
        self._sem_ctx = []

    def buf(self, name=""):
        return Buf(name)

    def bufs(self, n, name=""):
        return [Buf(f"{name}{i}") for i in range(n)]

    def _dep_waits(self, stream, reads, writes):
        waits = {}
        def need(src, val):
            if src == stream:
                waits[src] = max(waits.get(src, 0), val)
            else:
                waits[src] = max(waits.get(src, 0), val)
        for b in reads:
            if b.writer is not None:
                need(*b.writer)
        for b in writes:
            if b.writer is not None:
                need(*b.writer)
            for r in b.readers:
                need(*r)
        return waits

    def _record(self, stream, reads, writes):
        c = self.count[stream] + self.step[stream]
        self.count[stream] = c
        for b in reads:
            b.readers.append((stream, c))
        for b in writes:
            b.writer = (stream, c)
            b.readers = []
        return c

    def emit(self, stream, fn, reads=(), writes=(), skip_self_wait=False):
        assert stream in COMPUTE
        waits = self._dep_waits(stream, reads, writes)
        if skip_self_wait:
            waits.pop(stream, None)
        self._record(stream, reads, writes)
        self.ops[stream].append((waits, fn, stream))

    def dma(self, queue, fn, reads=(), writes=(), group=None):
        assert queue in QUEUES
        eng = QUEUES[queue]
        base = group if group is not None else queue
        wkey = writes[0].serial if writes else "ro"
        stream = f"{base}@{wkey}"
        if stream not in self.count:
            self.count[stream] = 0
            self.step[stream] = 16
        waits = self._dep_waits(stream, reads, writes)
        self._record(stream, reads, writes)
        self.ops[eng].append((waits, fn, stream))

    def collective(self, fn, reads=(), writes=()):
        waits = {s: c for s, c in self.count.items() if c > 0}
        c = self._record("gpsimd", list(reads), list(writes))
        self.ops["gpsimd"].append((waits, fn, "gpsimd"))
        fence = {"gpsimd": c}
        for s in ENGINES:
            self.ops[s].append((dict(fence), lambda e: None, s))

    def barrier_all(self):
        snap = dict(self.count)
        for s in COMPUTE:
            def mk(snap=snap):
                def fn(e):
                    return None
                return fn
            self.ops[s].append((dict(snap), mk(), s))

    def current(self, stream):
        return self.count[stream]

    def finalize(self, block, final_engine="sync"):
        nc = self.nc
        for s in list(self.count.keys()):
            if self.count[s] > 0 or s in ENGINES:
                self._sem_ctx.append(nc.semaphore(f"s_{self.tag}{s}"))
                self.sems[s] = self._sem_ctx[-1].__enter__()
        hw = {e: defaultdict(int) for e in ENGINES}

        def run_stream(e, engine_name):
            prev_inc_stream = None
            for waits, fn, inc_stream in self.ops[engine_name]:
                for src, val in sorted(waits.items()):
                    if src not in self.sems:
                        continue
                    if hw[engine_name][src] >= val:
                        continue
                    e.wait_ge(self.sems[src], val)
                    hw[engine_name][src] = val
                inst = fn(e)
                if inst is not None:
                    inst.then_inc(self.sems[inc_stream], self.step[inc_stream])
                    hw[engine_name][inc_stream] = 0 if False else hw[engine_name][inc_stream]

        for engine_name in ENGINES:
            if not self.ops[engine_name]:
                continue
            getattr(block, engine_name)(lambda e, en=engine_name: run_stream(e, en))

        fin = dict(self.count)
        def final_fn(e):
            for s, v in sorted(fin.items()):
                if v > 0 and s in self.sems:
                    e.wait_ge(self.sems[s], v)
        getattr(block, final_engine)(final_fn)

    def close(self):
        for c in reversed(self._sem_ctx):
            c.__exit__(None, None, None)


"""Block emitters for the megakernel.

Conventions:
- token index t in [0, 8192): t = b*4096 + l
- feature-major DRAM activations: [4, 128, 8192] (pt-tile j, partition=d, t)
- mamba d-slice: this core's 512 channels (host passes pre-sliced inputs)
"""
import concourse.mybir as mybir

F32 = mybir.dt.float32
F32R = mybir.dt.float32r
BF16 = mybir.dt.bfloat16
I32 = mybir.dt.int32
AF = mybir.ActivationFunctionType
ALU = mybir.AluOpType
AX = mybir.AxisListType

TC = 512
NCH = 4096 // TC
NTT = TC // 128


def emit_mamba(fw, nc, D, sb, ps):
    B = fw.buf

    xt = [sb(f"mx{i}", [128, 512], F32) for i in range(2)]
    bxt = fw.bufs(2, "xt")
    xlt = sb("mxl", [128, 512], F32); bxlt = B()
    sqt = sb("msq", [128, 512], F32); bsqt = B()
    stats1 = sb("mst1", [128, 64], F32); bst1 = B()
    stats2 = sb("mst2", [128, 64], F32); bst2 = B()
    statsl = sb("mstl", [128, 128], F32); bstatsl = B()
    mrow = sb("mmr", [128, 64], F32); bm = B()
    rrow = sb("mrr", [128, 64], F32); br = B()
    tmp64 = sb("mt64", [128, 64], F32); bt64 = B()
    wproj = sb("mxp", [128, 4 * 160], F32R); bwproj = B()
    wprojf = sb("mxpf", [128, 4 * 160], F32); bwprojf = B()
    dtw = sb("mdw", [128, 512], F32R); bdtw = B()
    dtwf = sb("mdwf", [128, 512], F32); bdtwf = B()
    dtb = sb("mdb", [128, 4], F32); bdtb = B()
    alog = sb("mal", [128, 64], F32); bal = B()
    a_sb = sb("ma", [128, 64], F32); ba = B()
    d_sb = sb("md", [128, 4], F32); bd = B()
    idn = sb("midn", [128, 128], F32); bidn = B()
    cv = sb("mcv", [128, 8], F32); bcv = B()
    ones_r = sb("mo1", [1, 128], F32R); bones = B()
    ones_f = sb("mo1f", [1, 128], F32); bones_f = B()
    xlnTc = sb("mxlc", [128, 4 * TC], F32R); bxlnTc = B()
    xTc = sb("mxTc", [128, 4 * TC], F32); bxTc = B()
    xpe1 = sb("mxe1", [128, TC], F32); bxpe1 = B()
    xpe2 = sb("mxe2", [32, TC], F32); bxpe2 = B()
    din = sb("mdin", [128, TC], F32R); bdin = B()
    dinf = sb("mdinf", [128, TC], F32); bdinf = B()
    sp1 = sb("msp1", [128, TC], F32); bsp1 = B()
    sp2 = sb("msp2", [128, TC], F32); bsp2 = B()
    dt_bf = sb("mdtbf", [128, TC], F32); bdt_bf = B()
    dtc = sb("mdtc", [128, 4 * TC], F32); bdtc = B()
    xlc = sb("mxlc2", [128, 4 * TC], F32R); bxlc = B()
    dtxc = sb("mdtx", [128, 4 * TC], F32); bdtx = B()
    bbc = [sb(f"mbbc{i}", [128, TC], F32) for i in range(2)]
    bbbc = fw.bufs(2, "mbbc")
    cbc = [sb(f"mcbc{i}", [128, TC], F32) for i in range(2)]
    bcbc = fw.bufs(2, "mcbc")
    bc32 = sb("mbc32", [32, 4096], F32); bbc32 = B()
    bchi = sb("mbchi", [32, 4096], BF16); bbchi = B()
    bclo = sb("mbclo", [32, 4096], BF16); bbclo = B()
    bclof = sb("mbclof", [32, 4096], F32); bbclof = B()
    bct = [sb(f"mbct{i}", [1, 4 * TC], BF16) for i in range(2)]
    bbct = fw.bufs(2, "bct")
    ones_b = sb("mo1b", [1, 128], BF16); bones_b = B()
    dA = [sb(f"mdA{i}", [128, TC], F32) for i in range(2)]
    bdA = fw.bufs(2, "dA")
    dB = [sb(f"mdB{i}", [128, TC], F32) for i in range(2)]
    bdB = fw.bufs(2, "dB")
    hts = [sb(f"mh{i}", [128, 4 * TC], F32) for i in range(2)]
    bh = fw.bufs(2, "h")
    zts = sb("mz", [128, 4 * TC], F32); bz = B()
    yts = sb("my", [128, 4 * TC], F32); by = B()
    carr = sb("mcar", [128, 64], F32); bcar = B()
    x1c = sb("mx1c", [128, 4 * TC], F32); bx1 = B()

    p_tr = [ps(f"mptr{i}", [128, 128], F32) for i in range(2)]
    bptr = fw.bufs(2, "ptr")
    p_xp1 = ps("mpxp1", [128, TC], F32); bpxp1 = B()
    p_xp2 = ps("mpxp2", [32, TC], F32); bpxp2 = B()
    p_bc = [ps(f"mpbc{i}", [128, TC], F32) for i in range(2)]
    bpbc = fw.bufs(2, "pbc")
    p_dt = ps("mpdt", [128, TC], F32); bpdt = B()

    b_stats_dram = B(); b_statsfull = B(); b_bcbf_dram = B()
    b_xp_dram = B(); b_xpfull = B()
    b_xT_dram = B(); b_xlnT_dram = B(); b_dt_dram = B(); b_x1_dram = B()

    fw.dma("qsync", lambda e: e.dma_start(idn[:], D["idn"]), writes=[bidn], group="s1")
    fw.dma("qsync", lambda e: e.dma_start(cv[:], D["constv"]), writes=[bcv], group="s2")
    fw.dma("qsync", lambda e: e.dma_start(
        wprojf[:].rearrange("p (j f) -> p j f", j=4),
        D["xprojT"].rearrange("(j p) f -> p j f", p=128)), writes=[bwprojf], group="s3")
    fw.dma("qsync", lambda e: e.dma_start(dtwf[:], D["dtwT"]), writes=[bdtwf], group="s4")
    fw.dma("qsync", lambda e: e.dma_start(dtb[:], D["dtb"]), writes=[bdtb], group="s5")
    fw.dma("qsync", lambda e: e.dma_start(
        alog[:].rearrange("p (j f) -> p j f", j=4),
        D["alog"].rearrange("(j p) f -> p j f", p=128)), writes=[bal], group="s6")
    fw.dma("qsync", lambda e: e.dma_start(d_sb[:], D["dvec"]), writes=[bd], group="s7")
    fw.dma("qsync", lambda e: e.dma_start(ones_f[:], D["onesv"]), writes=[bones_f], group="p0row")
    fw.emit("vector", lambda e: e.tensor_copy(ones_r[:], ones_f[:]),
            reads=[bones_f], writes=[bones])
    fw.emit("vector", lambda e: e.tensor_copy(ones_b[:], ones_f[:]),
            reads=[bones_f], writes=[bones_b])
    fw.emit("vector", lambda e: e.tensor_copy(wproj[:], wprojf[:]),
            reads=[bwprojf], writes=[bwproj])
    fw.emit("vector", lambda e: e.tensor_copy(dtw[:], dtwf[:]),
            reads=[bdtwf], writes=[bdtw])
    fw.emit("scalar", lambda e: e.activation(a_sb[:], alog[:], AF.Exp),
            reads=[bal], writes=[ba])
    fw.emit("vector", lambda e: e.tensor_scalar_mul(a_sb[:], a_sb[:], -1.0),
            reads=[ba], writes=[ba])

    def tcol(b, ch, tt):
        return b * 32 + ch * NTT + tt

    fw.emit("vector", lambda e: e.memset(stats1[:], 0.0), writes=[bst1])
    for b in range(2):
        for ch in range(NCH):
            for tt in range(NTT):
                slot = (ch * NTT + tt) % 2
                t0 = ch * TC + tt * 128
                col = tcol(b, ch, tt)
                fw.dma("qsync", lambda e, b=b, t0=t0, slot=slot: e.dma_start(
                    xt[slot][:], D["x_ds"][b, t0:t0 + 128, :]), writes=[bxt[slot]], group="s8")
                fw.emit("vector", lambda e, col=col, slot=slot: e.reduce_sum(
                    stats1[:, col:col + 1], xt[slot][:], axis=AX.X),
                    reads=[bxt[slot]], writes=[bst1], skip_self_wait=True)
                fw.emit("scalar", lambda e, col=col, slot=slot: e.activation(
                    sqt[:], xt[slot][:], AF.Square, accum_out=stats2[:, col:col + 1]),
                    reads=[bxt[slot]], writes=[bsqt, bst2], skip_self_wait=True)
    fw.dma("qgp", lambda e: e.dma_start(D["stats_part"][:, 0:64], stats1[:]),
           reads=[bst1], writes=[b_stats_dram], group="s9")
    fw.dma("qgp", lambda e: e.dma_start(D["stats_part"][:, 64:128], stats2[:]),
           reads=[bst2], writes=[b_stats_dram], group="s10")
    fw.collective(lambda e: e.collective_compute(
        "AllReduce", ALU.add, [list(range(8))], [D["stats_part"]], [D["stats_full"]]),
        reads=[b_stats_dram], writes=[b_statsfull])
    fw.dma("qgp", lambda e: e.dma_start(statsl[:], D["stats_full"]),
           reads=[b_statsfull], writes=[bstatsl], group="s11")
    fw.emit("vector", lambda e: e.tensor_scalar_mul(mrow[:], statsl[:, 0:64], 1.0 / 4096.0),
            reads=[bstatsl], writes=[bm])
    fw.emit("vector", lambda e: e.tensor_scalar_mul(tmp64[:], statsl[:, 64:128], 1.0 / 4096.0),
            reads=[bstatsl], writes=[bt64])
    fw.emit("scalar", lambda e: e.activation(rrow[:], mrow[:], AF.Square),
            reads=[bm], writes=[br])
    fw.emit("vector", lambda e: e.tensor_tensor(tmp64[:], tmp64[:], rrow[:], op=ALU.subtract),
            reads=[bt64, br], writes=[bt64])
    fw.emit("scalar", lambda e: e.activation(tmp64[:], tmp64[:], AF.Ln, bias=cv[:, 0:1]),
            reads=[bt64, bcv], writes=[bt64])
    fw.emit("scalar", lambda e: e.activation(rrow[:], tmp64[:], AF.Exp, scale=cv[:, 2:3]),
            reads=[bt64, bcv], writes=[br])

    for b in range(2):
        for ch in range(NCH):
            c0 = b * 4096 + ch * TC
            for tt in range(NTT):
                slot = tt % 2
                t0 = ch * TC + tt * 128
                col = tcol(b, ch, tt)
                fw.dma("qsync", lambda e, b=b, t0=t0, slot=slot: e.dma_start(
                    xt[slot][:], D["x_ds"][b, t0:t0 + 128, :]), writes=[bxt[slot]], group="s12")
                fw.emit("vector", lambda e, col=col, slot=slot: e.tensor_scalar(
                    xlt[:], xt[slot][:], mrow[:, col:col + 1], rrow[:, col:col + 1],
                    op0=ALU.subtract, op1=ALU.mult),
                    reads=[bxt[slot], bm, br], writes=[bxlt])
                for j in range(4):
                    pp = j % 2
                    dst = j * TC + tt * 128
                    fw.emit("tensor", lambda e, j=j, pp=pp: e.transpose(
                        p_tr[pp][:], xlt[:, j * 128:(j + 1) * 128], idn[:]),
                        reads=[bxlt, bidn], writes=[bptr[pp]])
                    fw.emit("scalar", lambda e, pp=pp, dst=dst: e.activation(
                        xlnTc[:, dst:dst + 128], p_tr[pp][:], AF.Copy),
                        reads=[bptr[pp]], writes=[bxlnTc], skip_self_wait=True)
                    fw.emit("tensor", lambda e, j=j, pp=pp, slot=slot: e.transpose(
                        p_tr[pp][:], xt[slot][:, j * 128:(j + 1) * 128], idn[:]),
                        reads=[bxt[slot], bidn], writes=[bptr[pp]])
                    fw.emit("scalar", lambda e, pp=pp, dst=dst: e.activation(
                        xTc[:, dst:dst + 128], p_tr[pp][:], AF.Copy),
                        reads=[bptr[pp]], writes=[bxTc], skip_self_wait=True)
            fw.dma("qgp", lambda e, c0=c0: e.dma_start(
                D["xlnT_dram"][:, :, c0:c0 + TC].transpose([1, 0, 2]),
                xlnTc[:].bitcast(F32).rearrange("p (j t) -> p j t", j=4)),
                reads=[bxlnTc], writes=[b_xlnT_dram], group="s13")
            fw.dma("qgp", lambda e, c0=c0: e.dma_start(
                D["xT_dram"][:, :, c0:c0 + TC].transpose([1, 0, 2]),
                xTc[:].rearrange("p (j t) -> p j t", j=4)),
                reads=[bxTc], writes=[b_xT_dram], group="s14")
            def mm_xp1(e):
                last = None
                for j in range(4):
                    last = e.matmul(p_xp1[:], wproj[:, j * 160:j * 160 + 128],
                                    xlnTc[:, j * TC:(j + 1) * TC],
                                    start=(j == 0), stop=(j == 3))
                return last
            def mm_xp2(e):
                last = None
                for j in range(4):
                    last = e.matmul(p_xp2[:], wproj[:, j * 160 + 128:j * 160 + 160],
                                    xlnTc[:, j * TC:(j + 1) * TC],
                                    start=(j == 0), stop=(j == 3))
                return last
            fw.emit("tensor", mm_xp1, reads=[bwproj, bxlnTc], writes=[bpxp1])
            fw.emit("tensor", mm_xp2, reads=[bwproj, bxlnTc], writes=[bpxp2])
            fw.emit("scalar", lambda e: e.activation(xpe1[:], p_xp1[:], AF.Copy),
                    reads=[bpxp1], writes=[bxpe1])
            fw.emit("scalar", lambda e: e.activation(xpe2[:], p_xp2[:], AF.Copy),
                    reads=[bpxp2], writes=[bxpe2])
            fw.dma("qgp", lambda e, c0=c0: e.dma_start(
                D["xp_part"][0:128, c0:c0 + TC], xpe1[:]),
                reads=[bxpe1], writes=[b_xp_dram], group="s15")
            fw.dma("qgp", lambda e, c0=c0: e.dma_start(
                D["xp_part"][128:160, c0:c0 + TC], xpe2[:]),
                reads=[bxpe2], writes=[b_xp_dram], group="p32st")

    fw.collective(lambda e: e.collective_compute(
        "AllReduce", ALU.add, [list(range(8))], [D["xp_part"]], [D["xp_full"]]),
        reads=[b_xp_dram], writes=[b_xpfull])

    b_bc2 = B("bc2d")
    for b in range(2):
        c0 = b * 4096
        fw.dma("qsync", lambda e, c0=c0: e.dma_start(
            bc32[:], D["xp_full"][128:160, c0:c0 + 4096]),
            reads=[b_xpfull], writes=[bbc32], group="p32ld")
        fw.emit("vector", lambda e: e.tensor_copy(bchi[:], bc32[:]),
                reads=[bbc32], writes=[bbchi])
        fw.emit("vector", lambda e: e.tensor_tensor(bclof[:], bc32[:], bchi[:], op=ALU.subtract),
                reads=[bbc32, bbchi], writes=[bbclof])
        fw.emit("vector", lambda e: e.tensor_copy(bclo[:], bclof[:]),
                reads=[bbclof], writes=[bbclo])
        for half, tile, btile in ((0, bchi, bbchi), (1, bclo, bbclo)):
            fw.dma("qgp", lambda e, c0=c0, half=half, tile=tile: e.dma_start(
                D["bc2"].rearrange("(n two) t -> two n t", two=2)[half][:, c0:c0 + 4096],
                tile[:]),
                reads=[btile], writes=[b_bc2], group="p32st")

    for b in range(2):
        for ch in range(NCH):
            c0 = b * 4096 + ch * TC
            fw.dma("qsync", lambda e, c0=c0: e.dma_start(
                dinf[:], D["xp_full"][0:128, c0:c0 + TC]),
                reads=[b_xpfull], writes=[bdinf], group="s16")
            fw.emit("vector", lambda e: e.tensor_copy(din[:], dinf[:]),
                    reads=[bdinf], writes=[bdin])
            for j in range(4):
                fw.emit("tensor", lambda e, j=j: e.matmul(
                    p_dt[:], dtw[:, j * 128:(j + 1) * 128], din[:],
                    start=True, stop=True),
                    reads=[bdtw, bdin], writes=[bpdt])
                fw.emit("scalar", lambda e, j=j: e.activation(
                    sp1[:], p_dt[:], AF.Abs, bias=dtb[:, j:j + 1]),
                    reads=[bpdt, bdtb], writes=[bsp1])
                fw.emit("scalar", lambda e: e.activation(
                    sp1[:], sp1[:], AF.Exp, scale=cv[:, 3:4]),
                    reads=[bsp1, bcv], writes=[bsp1])
                fw.emit("scalar", lambda e: e.activation(
                    sp1[:], sp1[:], AF.Ln, bias=cv[:, 1:2]),
                    reads=[bsp1, bcv], writes=[bsp1])
                fw.emit("scalar", lambda e, j=j: e.activation(
                    sp2[:], p_dt[:], AF.Relu, bias=dtb[:, j:j + 1]),
                    reads=[bpdt, bdtb], writes=[bsp2])
                fw.emit("vector", lambda e: e.tensor_tensor(
                    dt_bf[:], sp1[:], sp2[:], op=ALU.add),
                    reads=[bsp1, bsp2], writes=[bdt_bf])
                fw.dma("qgp", lambda e, j=j, c0=c0: e.dma_start(
                    D["dt_dram"][j, :, c0:c0 + TC], dt_bf[:]),
                    reads=[bdt_bf], writes=[b_dt_dram], group="s17")

        fw.emit("vector", lambda e: e.memset(carr[:], 0.0), writes=[bcar])
        for ch in range(NCH):
            c0 = b * 4096 + ch * TC
            fw.dma("qsync", lambda e, c0=c0: e.dma_start(
                dtc[:].rearrange("p (j t) -> p j t", j=4),
                D["dt_dram"][:, :, c0:c0 + TC].transpose([1, 0, 2])),
                reads=[b_dt_dram], writes=[bdtc], group="s18")
            fw.dma("qsync", lambda e, c0=c0: e.dma_start(
                xlc[:].bitcast(F32).rearrange("p (j t) -> p j t", j=4),
                D["xlnT_dram"][:, :, c0:c0 + TC].transpose([1, 0, 2])),
                reads=[b_xlnT_dram], writes=[bxlc], group="s19")
            fw.dma("qsync", lambda e, c0=c0: e.dma_start(
                xTc[:].rearrange("p (j t) -> p j t", j=4),
                D["xT_dram"][:, :, c0:c0 + TC].transpose([1, 0, 2])),
                reads=[b_xT_dram], writes=[bxTc], group="s20")
            fw.emit("vector", lambda e: e.tensor_tensor(
                dtxc[:], dtc[:], xlc[:].bitcast(F32), op=ALU.mult),
                reads=[bdtc, bxlc], writes=[bdtx])
            fw.emit("vector", lambda e: e.memset(yts[:], 0.0), writes=[by])
            for n in range(16):
                bsl = n % 2
                fw.dma("qscal", lambda e, n=n, c0=c0, bsl=bsl: e.dma_start(
                    bct[bsl][0:1, 0:2 * TC].rearrange("o (r t) -> o r t", r=2),
                    D["bc2"][2 * n:2 * n + 2, c0:c0 + TC].unsqueeze(0)),
                    reads=[b_bc2], writes=[bbct[bsl]], group="bct")
                fw.dma("qscal", lambda e, n=n, c0=c0, bsl=bsl: e.dma_start(
                    bct[bsl][0:1, 2 * TC:4 * TC].rearrange("o (r t) -> o r t", r=2),
                    D["bc2"][32 + 2 * n:32 + 2 * n + 2, c0:c0 + TC].unsqueeze(0)),
                    reads=[b_bc2], writes=[bbct[bsl]], group="bct")
                def mm_bbc(e, bsl=bsl):
                    e.matmul(p_bc[0][:], ones_b[:], bct[bsl][0:1, 0:TC],
                             start=True, stop=False)
                    return e.matmul(p_bc[0][:], ones_b[:], bct[bsl][0:1, TC:2 * TC],
                                    start=False, stop=True)
                fw.emit("tensor", mm_bbc, reads=[bones_b, bbct[bsl]], writes=[bpbc[0]])
                fw.emit("scalar", lambda e, bsl=bsl: e.activation(
                    bbc[bsl][:], p_bc[0][:], AF.Copy),
                    reads=[bpbc[0]], writes=[bbbc[bsl]])
                def mm_cbc(e, bsl=bsl):
                    e.matmul(p_bc[1][:], ones_b[:], bct[bsl][0:1, 2 * TC:3 * TC],
                             start=True, stop=False)
                    return e.matmul(p_bc[1][:], ones_b[:], bct[bsl][0:1, 3 * TC:4 * TC],
                                    start=False, stop=True)
                fw.emit("tensor", mm_cbc, reads=[bones_b, bbct[bsl]], writes=[bpbc[1]])
                fw.emit("scalar", lambda e, bsl=bsl: e.activation(
                    cbc[bsl][:], p_bc[1][:], AF.Copy),
                    reads=[bpbc[1]], writes=[bcbc[bsl]])
                if b == 0 and ch == 0 and n == 0 and "dbg_bct" in D:
                    fw.dma("qgp", lambda e, bsl=bsl: e.dma_start(
                        D["dbg_bct"], bct[bsl][:]), reads=[bbct[bsl]], writes=[B("d1")], group="s21")
                    fw.dma("qgp", lambda e, bsl=bsl: e.dma_start(
                        D["dbg_bbc"], bbc[bsl][:]), reads=[bbbc[bsl]], writes=[B("d2")], group="s22")
                    fw.dma("qgp", lambda e, bsl=bsl: e.dma_start(
                        D["dbg_cbc"], cbc[bsl][:]), reads=[bcbc[bsl]], writes=[B("d3")], group="s23")
                for j in range(4):
                    sl = (n * 4 + j) % 2
                    fw.emit("scalar", lambda e, j=j, n=n, sl=sl: e.activation(
                        dA[sl][:], dtc[:, j * TC:(j + 1) * TC], AF.Exp,
                        scale=a_sb[:, j * 16 + n:j * 16 + n + 1]),
                        reads=[bdtc, ba], writes=[bdA[sl]])
                    fw.emit("vector", lambda e, j=j, sl=sl, bsl=bsl: e.tensor_tensor(
                        dB[sl][:], dtxc[:, j * TC:(j + 1) * TC], bbc[bsl][:], op=ALU.mult),
                        reads=[bdtx, bbbc[bsl]], writes=[bdB[sl]])
                    hsl = (n * 4 + j) % 2
                    if ch == 0:
                        fw.emit("vector", lambda e, j=j, sl=sl, hsl=hsl: e.tensor_tensor_scan(
                            hts[hsl][:, j * TC:(j + 1) * TC], dA[sl][:], dB[sl][:],
                            0.0, op0=ALU.mult, op1=ALU.add),
                            reads=[bdA[sl], bdB[sl]], writes=[bh[hsl]])
                    else:
                        fw.emit("vector", lambda e, j=j, sl=sl, hsl=hsl, n=n: e.tensor_tensor_scan(
                            hts[hsl][:, j * TC:(j + 1) * TC], dA[sl][:], dB[sl][:],
                            carr[:, j * 16 + n:j * 16 + n + 1], op0=ALU.mult, op1=ALU.add),
                            reads=[bdA[sl], bdB[sl], bcar], writes=[bh[hsl]])
                    fw.emit("scalar", lambda e, j=j, n=n, hsl=hsl: e.activation(
                        carr[:, j * 16 + n:j * 16 + n + 1],
                        hts[hsl][:, j * TC + TC - 1:j * TC + TC], AF.Copy),
                        reads=[bh[hsl]], writes=[bcar], skip_self_wait=True)
                    fw.emit("vector", lambda e, j=j, hsl=hsl, bsl=bsl: e.tensor_tensor(
                        zts[:, j * TC:(j + 1) * TC], hts[hsl][:, j * TC:(j + 1) * TC],
                        cbc[bsl][:], op=ALU.mult),
                        reads=[bh[hsl], bcbc[bsl]], writes=[bz])
                fw.emit("gpsimd", lambda e: e.tensor_tensor(
                    yts[:], yts[:], zts[:], op=ALU.add),
                    reads=[bz, by], writes=[by])
            for j in range(4):
                fw.emit("vector", lambda e, j=j: e.scalar_tensor_tensor(
                    x1c[:, j * TC:(j + 1) * TC],
                    xlc[:, j * TC:(j + 1) * TC].bitcast(F32),
                    d_sb[:, j:j + 1], yts[:, j * TC:(j + 1) * TC],
                    op0=ALU.mult, op1=ALU.add),
                    reads=[bxlc, bd, by], writes=[bx1], skip_self_wait=True)
            fw.emit("vector", lambda e: e.tensor_tensor(
                x1c[:], x1c[:], xTc[:], op=ALU.add),
                reads=[bx1, bxTc], writes=[bx1])
            fw.dma("qgp", lambda e, c0=c0: e.dma_start(
                D["x1_out"].rearrange("(j p) t -> p j t", p=128)[:, :, c0:c0 + TC],
                x1c[:].rearrange("p (j t) -> p j t", j=4)),
                reads=[bx1], writes=[b_x1_dram], group="s24")
    return dict(x1_dram=b_x1_dram)


NIT = 86
NIT_PAD = 88
NHT = 32


def emit_wconv(fw, nc, D, sb, ps):
    B = fw.buf
    wi = [sb(f"wci{i}", [128, 1408], I32) for i in range(2)]
    bwi = fw.bufs(2, "wci")
    wb = [sb(f"wcb{i}", [128, 1408], BF16) for i in range(2)]
    bwb = fw.bufs(2, "wcb")
    b_loc = {k: B(k) for k in ("wg", "wu", "wd")}
    b_ag = {k: B(k + "ag") for k in ("wg", "wu", "wd")}

    for key, src, loc, rows, cols in (
        ("wg", "wgT_s", "wg_loc", 32, 1408),
        ("wu", "wuT_s", "wu_loc", 32, 1408),
        ("wd", "wdT_s", "wd_loc", 86, 512),
    ):
        for rr in range(rows):
            sl = rr % 2
            r0 = rr * 128
            fw.dma("qsync", lambda e, src=src, r0=r0, sl=sl, cols=cols: e.dma_start(
                wi[sl][:, :cols], D[src][r0:r0 + 128, :]), writes=[bwi[sl]], group="s25")
            fw.emit("vector", lambda e, sl=sl, cols=cols: e.tensor_copy(
                wb[sl][:, :cols], wi[sl][:, :cols]),
                reads=[bwi[sl]], writes=[bwb[sl]])
            fw.dma("qgp", lambda e, loc=loc, r0=r0, sl=sl, cols=cols: e.dma_start(
                D[loc][r0:r0 + 128, :], wb[sl][:, :cols]),
                reads=[bwb[sl]], writes=[b_loc[key]], group="s26")
        import os as _os3
        if _os3.environ.get("SKIP_AG") != "1":
            fw.emit("gpsimd", lambda e, key=key, loc=loc: e.collective_compute(
                "AllGather", ALU.bypass, [list(range(8))], [D[loc]], [D[key + "_ag"]]),
                reads=[b_loc[key]], writes=[b_ag[key]])
    return b_ag


def emit_l1ffn(fw, nc, D, sb, ps, b_ag=None):
    B = fw.buf
    bag = b_ag or {}

    xt = [sb(f"fx{i}", [128, 1024], F32) for i in range(2)]
    bxt = fw.bufs(2, "fx")
    xsq = sb("fxsq", [128, 1024], F32); bxsq = B()
    onecf = sb("fonecf", [128, 1], F32); bonecf = B()
    ones_f = sb("fof", [1, 128], F32); bof = B()
    mrow = sb("fm", [1, 1024], F32); bm = B()
    rrow = sb("fr", [1, 1024], F32); br = B()
    t1 = sb("ft1", [1, 1024], F32); bt1 = B()
    mb = sb("fmb", [128, 1024], F32); bmb = B()
    rb = sb("frb", [128, 1024], F32); brb = B()
    cv = sb("fcv", [128, 8], F32); bcv = B()
    xln = [sb(f"fxl{j}", [128, 1024], BF16) for j in range(32)]
    bxln = fw.bufs(32, "fxl")
    sg_t = sb("fsg", [128, 86], F32); bsg = B()
    su_t = sb("fsu", [128, 86], F32); bsu = B()
    sd_t = sb("fsd", [128, 32], F32); bsd = B()
    wg_sb = [sb(f"fwg{i}", [128, 32 * 128], BF16) for i in range(2)]
    bwg = fw.bufs(2, "fwg")
    wu_sb = [sb(f"fwu{i}", [128, 32 * 128], BF16) for i in range(2)]
    bwu = fw.bufs(2, "fwu")
    sgact = sb("fsga", [128, 512], F32); bsga = B()
    hout = [sb(f"fho{i}", [128, 512], BF16) for i in range(2)]
    bho = fw.bufs(2, "fho")
    hin = [sb(f"fhi{i}", [128, 512], BF16) for i in range(2)]
    bhi = fw.bufs(2, "fhi")
    wd_sb = [sb(f"fwd{i}", [128, 8 * 128], BF16) for i in range(2)]
    bwd = fw.bufs(2, "fwd")
    oev = sb("foev", [128, 512], F32); boev = B()
    xtmp = sb("fxtmp", [128, 1024], F32); bxtmp = B()
    x3t = sb("fx3t", [128, 512], F32); bx3t = B()

    P = [ps(f"fp{i}", [128, 512], F32) for i in range(8)]
    bP = fw.bufs(8, "fp")
    b_hd = B("hactd"); b_out = B("outfm")

    fw.dma("qsync", lambda e: e.dma_start(cv[:], D["constv"]), writes=[bcv], group="s27")
    fw.dma("qsync", lambda e: e.dma_start(ones_f[:], D["onesv"]), writes=[bof], group="p0row")
    fw.dma("qsync", lambda e: e.dma_start(onecf[:], D["onescol"]), writes=[bonecf], group="s28")
    fw.dma("qsync", lambda e: e.dma_start(sg_t[:], D["sg_t"]), writes=[bsg], group="s29")
    fw.dma("qsync", lambda e: e.dma_start(su_t[:], D["su_t"]), writes=[bsu], group="s30")
    fw.dma("qsync", lambda e: e.dma_start(sd_t[:], D["sd_t"]), writes=[bsd], group="s31")

    for j in range(32):
        sl = j % 2
        fw.dma("qsync", lambda e, j=j, sl=sl: e.dma_start(
            xt[sl][:], D["x3_fm"][j]), writes=[bxt[sl]], group="s32")
        fw.emit("scalar", lambda e, sl=sl: e.activation(xsq[:], xt[sl][:], AF.Square),
                reads=[bxt[sl]], writes=[bxsq])
        for h in range(2):
            fw.emit("tensor", lambda e, j=j, h=h, sl=sl: e.matmul(
                P[h][0:1, :], onecf[:], xt[sl][:, h * 512:(h + 1) * 512],
                start=(j == 0), stop=(j == 31)),
                reads=[bxt[sl], bonecf], writes=[bP[h]], skip_self_wait=True)
            fw.emit("tensor", lambda e, j=j, h=h: e.matmul(
                P[2 + h][0:1, :], onecf[:], xsq[:, h * 512:(h + 1) * 512],
                start=(j == 0), stop=(j == 31)),
                reads=[bxsq, bonecf], writes=[bP[2 + h]], skip_self_wait=True)
    for h in range(2):
        fw.emit("vector", lambda e, h=h: e.tensor_scalar_mul(
            mrow[0:1, h * 512:(h + 1) * 512], P[h][0:1, :], 1.0 / 4096.0),
            reads=[bP[h]], writes=[bm], skip_self_wait=True)
        fw.emit("vector", lambda e, h=h: e.tensor_scalar_mul(
            t1[0:1, h * 512:(h + 1) * 512], P[2 + h][0:1, :], 1.0 / 4096.0),
            reads=[bP[2 + h]], writes=[bt1], skip_self_wait=True)
    fw.emit("scalar", lambda e: e.activation(rrow[:], mrow[:], AF.Square),
            reads=[bm], writes=[br])
    fw.emit("vector", lambda e: e.tensor_tensor(t1[:], t1[:], rrow[:], op=ALU.subtract),
            reads=[bt1, br], writes=[bt1])
    fw.emit("scalar", lambda e: e.activation(t1[:], t1[:], AF.Ln, bias=cv[0:1, 0:1]),
            reads=[bt1, bcv], writes=[bt1])
    fw.emit("scalar", lambda e: e.activation(rrow[:], t1[:], AF.Exp, scale=cv[0:1, 2:3]),
            reads=[bt1, bcv], writes=[br])
    for h in range(2):
        fw.emit("tensor", lambda e, h=h: e.matmul(
            P[4][:], ones_f[:], mrow[0:1, h * 512:(h + 1) * 512], start=True, stop=True),
            reads=[bof, bm], writes=[bP[4]])
        fw.emit("scalar", lambda e, h=h: e.activation(
            mb[:, h * 512:(h + 1) * 512], P[4][:], AF.Copy),
            reads=[bP[4]], writes=[bmb], skip_self_wait=True)
        fw.emit("tensor", lambda e, h=h: e.matmul(
            P[4][:], ones_f[:], rrow[0:1, h * 512:(h + 1) * 512], start=True, stop=True),
            reads=[bof, br], writes=[bP[4]])
        fw.emit("scalar", lambda e, h=h: e.activation(
            rb[:, h * 512:(h + 1) * 512], P[4][:], AF.Copy),
            reads=[bP[4]], writes=[brb], skip_self_wait=True)
    for j in range(32):
        sl = j % 2
        fw.dma("qsync", lambda e, j=j, sl=sl: e.dma_start(
            xt[sl][:], D["x3_fm"][j]), writes=[bxt[sl]], group="s33")
        fw.emit("vector", lambda e, sl=sl: e.tensor_tensor(
            xtmp[:], xt[sl][:], mb[:], op=ALU.subtract),
            reads=[bxt[sl], bmb], writes=[bxtmp])
        fw.emit("vector", lambda e, j=j: e.tensor_tensor(
            xln[j][:], xtmp[:], rb[:], op=ALU.mult),
            reads=[bxtmp, brb], writes=[bxln[j]])

    for it in range(NIT):
        r = it % 8
        jl = it // 8
        sl = it % 2
        fw.dma("qgp", lambda e, it=it, sl=sl: e.dma_start(
            wg_sb[sl][:].rearrange("p (k f) -> p k f", k=32),
            D["wgT"][:, it * 128:(it + 1) * 128]
            .rearrange("(k p) f -> p k f", p=128)),
            writes=[bwg[sl]], group="s34")
        fw.dma("qgp", lambda e, it=it, sl=sl: e.dma_start(
            wu_sb[sl][:].rearrange("p (k f) -> p k f", k=32),
            D["wuT"][:, it * 128:(it + 1) * 128]
            .rearrange("(k p) f -> p k f", p=128)),
            writes=[bwu[sl]], group="s35")
        for h in range(2):
            pg, pu = 4 + h, 6 + h
            def mm_g(e, sl=sl, h=h, pg=pg):
                last = None
                for k in range(32):
                    last = e.matmul(P[pg][:], wg_sb[sl][:, k * 128:(k + 1) * 128],
                                    xln[k][:, h * 512:(h + 1) * 512],
                                    start=(k == 0), stop=(k == 31))
                return last
            def mm_u(e, sl=sl, h=h, pu=pu):
                last = None
                for k in range(32):
                    last = e.matmul(P[pu][:], wu_sb[sl][:, k * 128:(k + 1) * 128],
                                    xln[k][:, h * 512:(h + 1) * 512],
                                    start=(k == 0), stop=(k == 31))
                return last
            fw.emit("tensor", mm_g, reads=[bwg[sl]] + bxln, writes=[bP[pg]])
            fw.emit("tensor", mm_u, reads=[bwu[sl]] + bxln, writes=[bP[pu]])
            fw.emit("scalar", lambda e, it=it, pg=pg: e.activation(
                sgact[:], P[pg][:], AF.Silu, scale=sg_t[:, it:it + 1]),
                reads=[bP[pg], bsg], writes=[bsga])
            hsl = h
            fw.emit("vector", lambda e, it=it, pu=pu, hsl=hsl: e.scalar_tensor_tensor(
                hout[hsl][:], P[pu][:], su_t[:, it:it + 1], sgact[:],
                op0=ALU.mult, op1=ALU.mult),
                reads=[bP[pu], bsu, bsga], writes=[bho[hsl]])
            fw.dma("qgp", lambda e, it=it, h=h, hsl=hsl: e.dma_start(
                D["hact_dram"][h, it], hout[hsl][:]),
                reads=[bho[hsl]], writes=[b_hd], group="s36")

    for half in range(2):
        for grp in range(4):
            for k in range(NIT):
                sl = k % 2
                fw.dma("qgp", lambda e, grp=grp, k=k, sl=sl: e.dma_start(
                    wd_sb[sl][:],
                    D["wdT"][128 * k:128 * k + 128, grp * 1024:(grp + 1) * 1024]),
                    writes=[bwd[sl]], group="s37")
                fw.dma("qscal", lambda e, half=half, k=k, sl=sl: e.dma_start(
                    hin[sl][:], D["hact_dram"][half, k]),
                    reads=[b_hd], writes=[bhi[sl]], group="s38")
                def mm_d(e, k=k, sl=sl):
                    last = None
                    for g in range(8):
                        last = e.matmul(P[g][:], wd_sb[sl][:, g * 128:(g + 1) * 128],
                                        hin[sl][:],
                                        start=(k == 0), stop=(k == NIT - 1))
                    return last
                fw.emit("tensor", mm_d, reads=[bwd[sl], bhi[sl]],
                        writes=[bP[g2] for g2 in range(8)], skip_self_wait=True)
            for g in range(8):
                ht = grp * 8 + g
                fw.dma("qscal", lambda e, ht=ht, half=half: e.dma_start(
                    x3t[:], D["x3_fm"][ht, :, half * 512:(half + 1) * 512]),
                    writes=[bx3t], group="s39")
                fw.emit("vector", lambda e, ht=ht, g=g: e.scalar_tensor_tensor(
                    oev[:], P[g][:], sd_t[:, ht:ht + 1], x3t[:],
                    op0=ALU.mult, op1=ALU.add),
                    reads=[bP[g], bsd, bx3t], writes=[boev])
                fw.dma("qgp", lambda e, ht=ht, half=half: e.dma_start(
                    D["out_fm"][ht, :, half * 512:(half + 1) * 512], oev[:]),
                    reads=[boev], writes=[b_out], group="s40")
    return dict(out=b_out)


def emit_attn(fw, nc, D, sb, ps):
    B = fw.buf
    bgroups = [[0, 1, 2, 3], [4, 5, 6, 7]]

    xt = [sb(f"ax{i}", [128, 1024], F32) for i in range(2)]
    bxt = fw.bufs(2, "ax")
    xsq = sb("axsq", [128, 1024], F32); bxsq = B()
    onecf = sb("aonecf", [128, 1], F32); bonecf = B()
    ones_f = sb("aof", [1, 128], F32); bof = B()
    mrow = sb("am", [1, 1024], F32); bm = B()
    rrow = sb("ar", [1, 1024], F32); br = B()
    t1 = sb("at1", [1, 1024], F32); bt1 = B()
    mb = sb("amb", [128, 1024], F32); bmb = B()
    rb = sb("arb", [128, 1024], F32); brb = B()
    cv = sb("acv", [128, 8], F32); bcv = B()
    xtmp = sb("axtmp", [128, 1024], F32); bxtmp = B()
    xln = [sb(f"axl{j}", [128, 1024], BF16) for j in range(2)]
    bxln = fw.bufs(2, "axl")
    wqkv_sb = sb("awqkv", [128, 32 * 16], BF16); bwqkv = B()
    wqkv_f = sb("awqkvf", [128, 32 * 16], F32); bwqkvf = B()
    bqkv_sb = sb("abqkv", [16, 1], F32); bbqkv = B()
    cqkv = sb("acqkv", [16, 1024], F32); bcqkv = B()
    cq_r = sb("acqr", [12, 1024], F32R); bcq_r = B()
    wqk_sb = sb("awqk", [12, 128], F32); bwqk = B()
    wqk_r = sb("awqkr", [12, 128], F32R); bwqk_r = B()
    bqk_sb = sb("abqk", [128, 1], F32); bbqk = B()
    bqkh = sb("abqkh", [128, 1], F32); bbqkh = B()
    q_sb = sb("aq", [128, 1024], F32); bq = B()
    qabs = sb("aqabs", [128, 1024], F32); bqabs = B()
    mcol = sb("amcol", [128, 1], F32); bmcol = B()
    e4t_sb = sb("ae4t", [32, 128], F32); be4t = B()
    mrow4 = sb("amrow4", [4, 1], F32); bmrow4 = B()
    crow = sb("acrow", [32, 1024], F32); bcrow = B()
    ckv_sb = sb("ackv", [128, 4096], F32); bckv = B()
    ckv_r = sb("ackvr", [128, 4096], F32R); bckv_r = B()
    idn_a = sb("aidn", [128, 128], F32); bidn_a = B()
    e4_sb = sb("ae4", [128, 32], F32); be4 = B()
    ckvT = sb("ackvT", [128, 32 * 5], F32R); bckvT = B()
    ckvT_f = sb("ackvTf", [128, 5], F32); bckvT_f = B()
    qasm = [sb(f"aqasm{i}", [128, 512], F32R) for i in range(2)]
    bqasm = fw.bufs(2, "aqasm")
    qasm_f = [sb(f"aqasmf{i}", [128, 512], F32) for i in range(2)]
    bqasm_f = fw.bufs(2, "aqasmf")
    P_sb = [sb(f"aP{i}", [128, 512], F32R) for i in range(2)]
    bP_sb = fw.bufs(2, "aP")
    olat_h = sb("aolat", [5, 512], F32); bolat = B()
    olat_all = sb("aolall", [128, 1024], F32); bolall = B()
    den_all = sb("aden", [32, 1024], F32); bden = B()
    denb = sb("adenb", [128, 1024], F32); bdenb = B()
    olat_bf = sb("aolbf", [128, 1024], BF16); bolbf = B()
    wvuT_sb = sb("awvuT", [128, 16], F32); bwvuT = B()
    wvuT_b = sb("awvuTb", [128, 16], BF16); bwvuT_b = B()
    wo_sb = [sb(f"awo{i}", [128, 4096], BF16) for i in range(4)]
    bwo = fw.bufs(4, "awo")
    wcev = sb("awcev", [4, 4096], BF16); bwcev = B()
    boev = sb("aboev", [1, 4096], F32); bboev = B()
    bvu_sb = sb("abvu", [128, 4], F32); bbvu = B()
    bvu_b = sb("abvub", [128, 4], BF16); bbvu_b = B()
    wcomb_sb = sb("awcomb", [128, 4096], BF16); bwcomb = B()
    bout_sb = sb("about", [128, 32], F32); bbout = B()
    bo_sb = sb("abo", [128, 32], F32); bbo = B()
    oev = sb("aoev", [128, 512], F32); boev2 = B()
    oev2 = sb("aoev2", [128, 512], F32); boev22 = B()

    P = [ps(f"ap{i}", [128, 512], F32) for i in range(8)]
    bP = fw.bufs(8, "apb")
    b_ckvd = B(); b_ckvag = B(); b_wcd = B(); b_wcag = B()
    b_bod = B(); b_bofull = B(); b_x3 = B(); b_m4d = B()

    fw.dma("qsync", lambda e: e.dma_start(cv[:], D["constv"]), writes=[bcv], group="s41")
    fw.dma("qsync", lambda e: e.dma_start(ones_f[:], D["onesv"]), writes=[bof], group="p0row")
    fw.dma("qsync", lambda e: e.dma_start(onecf[:], D["onescol"]), writes=[bonecf], group="s42")
    fw.dma("qsync", lambda e: e.dma_start(
        wqkv_f[:].rearrange("p (k f) -> p k f", k=32),
        D["wqkv"].rearrange("(k p) f -> p k f", p=128)), writes=[bwqkvf], group="s43")
    fw.dma("qsync", lambda e: e.dma_start(bqkv_sb[:], D["bqkv"]), writes=[bbqkv], group="p16")
    fw.dma("qsync", lambda e: e.dma_start(wqk_sb[:], D["w_qk"]), writes=[bwqk], group="p12")
    fw.dma("qsync", lambda e: e.dma_start(bqk_sb[:], D["b_qk"]), writes=[bbqk], group="s44")
    fw.dma("qsync", lambda e: e.dma_start(e4t_sb[:], D["e4t"]), writes=[be4t], group="p32ld")
    fw.dma("qsync", lambda e: e.dma_start(e4_sb[:], D["e4"]), writes=[be4], group="s45")
    fw.dma("qsync", lambda e: e.dma_start(idn_a[:], D["idn"]), writes=[bidn_a], group="s46")
    fw.emit("vector", lambda e: e.memset(qasm_f[0][:], 0.0), writes=[bqasm_f[0]])
    fw.emit("vector", lambda e: e.memset(qasm_f[1][:], 0.0), writes=[bqasm_f[1]])
    fw.emit("vector", lambda e: e.tensor_copy(qasm[0][:], qasm_f[0][:]),
            reads=[bqasm_f[0]], writes=[bqasm[0]])
    fw.emit("vector", lambda e: e.tensor_copy(qasm[1][:], qasm_f[1][:]),
            reads=[bqasm_f[1]], writes=[bqasm[1]])
    fw.dma("qsync", lambda e: e.dma_start(bvu_sb[:], D["bvu_s"]), writes=[bbvu], group="s47")
    fw.dma("qsync", lambda e: e.dma_start(
        wvuT_sb[:].rearrange("p (k f) -> p k f", k=4),
        D["wvuT_s"]), writes=[bwvuT], group="s48")
    fw.dma("qsync", lambda e: e.dma_start(bo_sb[:], D["bo_t"]), writes=[bbo], group="s49")
    fw.emit("vector", lambda e: e.tensor_copy(wqkv_sb[:], wqkv_f[:]),
            reads=[bwqkvf], writes=[bwqkv])
    fw.emit("vector", lambda e: e.tensor_copy(wqk_r[:], wqk_sb[:]),
            reads=[bwqk], writes=[bwqk_r])
    fw.emit("vector", lambda e: e.tensor_scalar_mul(bqkh[:], bqk_sb[:], 0.5),
            reads=[bbqk], writes=[bbqkh])
    fw.emit("vector", lambda e: e.tensor_copy(wvuT_b[:], wvuT_sb[:]),
            reads=[bwvuT], writes=[bwvuT_b])
    fw.emit("vector", lambda e: e.tensor_copy(bvu_b[:], bvu_sb[:]),
            reads=[bbvu], writes=[bbvu_b])

    for hh in range(4):
        fw.dma("qgp", lambda e, hh=hh: e.dma_start(
            wo_sb[hh][:], D["wo_rows"][hh * 128:(hh + 1) * 128, :]), writes=[bwo[hh]], group="s50")
    for hh in range(4):
        for c in range(8):
            pc = c % 2
            fw.emit("tensor", lambda e, hh=hh, c=c, pc=pc: e.matmul(
                P[pc][0:4, :], wvuT_b[:, hh * 4:(hh + 1) * 4],
                wo_sb[hh][:, c * 512:(c + 1) * 512], start=True, stop=True),
                reads=[bwvuT_b, bwo[hh]], writes=[bP[pc]])
            fw.emit("scalar", lambda e, c=c, pc=pc: e.activation(
                wcev[0:4, c * 512:(c + 1) * 512], P[pc][0:4, :], AF.Copy),
                reads=[bP[pc]], writes=[bwcev], skip_self_wait=True)
        fw.dma("qgp", lambda e, hh=hh: e.dma_start(
            D["wcomb_loc"][hh * 4:(hh + 1) * 4, :], wcev[0:4, :]),
            reads=[bwcev], writes=[b_wcd], group="p4st")
    for c in range(8):
        pc = 2 + c % 2
        for hh in range(4):
            fw.emit("tensor", lambda e, hh=hh, c=c, pc=pc: e.matmul(
                P[pc][0:1, :], bvu_b[:, hh:hh + 1],
                wo_sb[hh][:, c * 512:(c + 1) * 512],
                start=(hh == 0), stop=(hh == 3)),
                reads=[bbvu_b, bwo[hh]], writes=[bP[pc]], skip_self_wait=True)
        fw.emit("scalar", lambda e, c=c, pc=pc: e.activation(
            boev[0:1, c * 512:(c + 1) * 512], P[pc][0:1, :], AF.Copy),
            reads=[bP[pc]], writes=[bboev], skip_self_wait=True)
    fw.dma("qgp", lambda e: e.dma_start(D["bout_part"], boev[:]),
           reads=[bboev], writes=[b_bod], group="p0st")

    for j in range(32):
        sl = j % 2
        fw.dma("qsync", lambda e, j=j, sl=sl: e.dma_start(
            xt[sl][:], D["x2_fm"][j]), writes=[bxt[sl]], group="s51")
        fw.emit("scalar", lambda e, sl=sl: e.activation(xsq[:], xt[sl][:], AF.Square),
                reads=[bxt[sl]], writes=[bxsq])
        for h in range(2):
            fw.emit("tensor", lambda e, j=j, h=h, sl=sl: e.matmul(
                P[4 + h][0:1, :], onecf[:], xt[sl][:, h * 512:(h + 1) * 512],
                start=(j == 0), stop=(j == 31)),
                reads=[bxt[sl], bonecf], writes=[bP[4 + h]], skip_self_wait=True)
            fw.emit("tensor", lambda e, j=j, h=h: e.matmul(
                P[6 + h][0:1, :], onecf[:], xsq[:, h * 512:(h + 1) * 512],
                start=(j == 0), stop=(j == 31)),
                reads=[bxsq, bonecf], writes=[bP[6 + h]], skip_self_wait=True)
    for h in range(2):
        fw.emit("vector", lambda e, h=h: e.tensor_scalar_mul(
            mrow[0:1, h * 512:(h + 1) * 512], P[4 + h][0:1, :], 1.0 / 4096.0),
            reads=[bP[4 + h]], writes=[bm], skip_self_wait=True)
        fw.emit("vector", lambda e, h=h: e.tensor_scalar_mul(
            t1[0:1, h * 512:(h + 1) * 512], P[6 + h][0:1, :], 1.0 / 4096.0),
            reads=[bP[6 + h]], writes=[bt1], skip_self_wait=True)
    fw.emit("scalar", lambda e: e.activation(rrow[:], mrow[:], AF.Square),
            reads=[bm], writes=[br])
    fw.emit("vector", lambda e: e.tensor_tensor(t1[:], t1[:], rrow[:], op=ALU.subtract),
            reads=[bt1, br], writes=[bt1])
    fw.emit("scalar", lambda e: e.activation(t1[:], t1[:], AF.Ln, bias=cv[0:1, 0:1]),
            reads=[bt1, bcv], writes=[bt1])
    fw.emit("scalar", lambda e: e.activation(rrow[:], t1[:], AF.Exp, scale=cv[0:1, 2:3]),
            reads=[bt1, bcv], writes=[br])
    for h in range(2):
        fw.emit("tensor", lambda e, h=h: e.matmul(
            P[4][:], ones_f[:], mrow[0:1, h * 512:(h + 1) * 512], start=True, stop=True),
            reads=[bof, bm], writes=[bP[4]])
        fw.emit("scalar", lambda e, h=h: e.activation(
            mb[:, h * 512:(h + 1) * 512], P[4][:], AF.Copy),
            reads=[bP[4]], writes=[bmb], skip_self_wait=True)
        fw.emit("tensor", lambda e, h=h: e.matmul(
            P[4][:], ones_f[:], rrow[0:1, h * 512:(h + 1) * 512], start=True, stop=True),
            reads=[bof, br], writes=[bP[4]])
        fw.emit("scalar", lambda e, h=h: e.activation(
            rb[:, h * 512:(h + 1) * 512], P[4][:], AF.Copy),
            reads=[bP[4]], writes=[brb], skip_self_wait=True)
    for j in range(32):
        sl = j % 2
        fw.dma("qsync", lambda e, j=j, sl=sl: e.dma_start(
            xt[sl][:], D["x2_fm"][j]), writes=[bxt[sl]], group="s52")
        fw.emit("vector", lambda e, sl=sl: e.tensor_tensor(
            xtmp[:], xt[sl][:], mb[:], op=ALU.subtract),
            reads=[bxt[sl], bmb], writes=[bxtmp])
        fw.emit("vector", lambda e, j=j, sl=sl: e.tensor_tensor(
            xln[sl][:], xtmp[:], rb[:], op=ALU.mult),
            reads=[bxtmp, brb], writes=[bxln[sl]])
        for h in range(2):
            fw.emit("tensor", lambda e, j=j, h=h, sl=sl: e.matmul(
                P[h][0:16, :], wqkv_sb[:, j * 16:(j + 1) * 16],
                xln[sl][:, h * 512:(h + 1) * 512],
                start=(j == 0), stop=(j == 31)),
                reads=[bwqkv, bxln[sl]], writes=[bP[h]], skip_self_wait=True)

    for h in range(2):
        fw.emit("scalar", lambda e, h=h: e.activation(
            cqkv[0:16, h * 512:(h + 1) * 512], P[h][0:16, :], AF.Identity,
            bias=bqkv_sb[:]),
            reads=[bP[h], bbqkv], writes=[bcqkv], skip_self_wait=True)
    fw.dma("qgp", lambda e: e.dma_start(D["ckv_part"], cqkv[12:16, :]),
           reads=[bcqkv], writes=[b_ckvd], group="p12st")
    fw.collective(lambda e: e.collective_compute(
        "AllGather", ALU.bypass, bgroups, [D["ckv_part"]], [D["ckv_ag"]]),
        reads=[b_ckvd], writes=[b_ckvag])
    fw.collective(lambda e: e.collective_compute(
        "AllGather", ALU.bypass, [list(range(8))], [D["wcomb_loc"]], [D["wcomb_ag"]]),
        reads=[b_wcd], writes=[b_wcag])
    fw.collective(lambda e: e.collective_compute(
        "AllReduce", ALU.add, [list(range(8))], [D["bout_part"]], [D["bout_full"]]),
        reads=[b_bod], writes=[b_bofull])
    fw.dma("qsync", lambda e: e.dma_start(wcomb_sb[:], D["wcomb_ag"]),
           reads=[b_wcag], writes=[bwcomb], group="s53")
    fw.dma("qsync", lambda e: e.dma_start(
        bout_sb[:], D["bout_full"].rearrange("o (j p) -> p (o j)", p=128),
        allow_slow_non_contiguous=True),
        reads=[b_bofull], writes=[bbout], group="s54")
    fw.emit("vector", lambda e: e.memset(ckv_sb[:], 0.0), writes=[bckv])
    fw.dma("qsync", lambda e: e.dma_start(ckv_sb[4:5, :], D["ones4k"]),
           reads=[bckv], writes=[bckv], group="p4b")
    for rr in range(4):
        fw.dma("qsync", lambda e, rr=rr: e.dma_start(
            ckv_sb[0:4, rr * 1024:(rr + 1) * 1024], D["ckv_ag"][rr * 4:(rr + 1) * 4, :]),
            reads=[b_ckvag], writes=[bckv], group="p4ld")
    fw.emit("vector", lambda e: e.tensor_copy(ckv_r[:], ckv_sb[:]),
            reads=[bckv], writes=[bckv_r])
    for kt in range(32):
        pc = kt % 2
        fw.emit("tensor", lambda e, kt=kt, pc=pc: e.transpose(
            P[pc][0:128, 0:5], ckv_sb[0:5, kt * 128:(kt + 1) * 128], idn_a[0:5, 0:5]),
            reads=[bckv, bidn_a], writes=[bP[pc]])
        fw.emit("scalar", lambda e, kt=kt, pc=pc: e.activation(
            ckvT[:, kt * 5:(kt + 1) * 5], P[pc][0:128, 0:5], AF.Copy),
            reads=[bP[pc]], writes=[bckvT], skip_self_wait=True)
    fw.emit("vector", lambda e: e.reduce_max(
        mrow4[:], ckv_sb[0:4, :], axis=AX.X, apply_absolute_value=True),
        reads=[bckv], writes=[bmrow4])
    fw.dma("qgp", lambda e: e.dma_start(D["m4_dram"], mrow4[:]),
           reads=[bmrow4], writes=[b_m4d], group="p4st")
    for h32 in range(32):
        fw.dma("qsync", lambda e, h32=h32: e.dma_start(
            mcol[h32 * 4:(h32 + 1) * 4, :], D["m4_dram"]),
            reads=[b_m4d], writes=[bmcol], group="mcol")

    fw.emit("vector", lambda e: e.tensor_copy(cq_r[:], cqkv[0:12, :]),
            reads=[bcqkv], writes=[bcq_r])
    for h in range(2):
        fw.emit("tensor", lambda e, h=h: e.matmul(
            P[2 + h][:], wqk_r[:], cq_r[:, h * 512:(h + 1) * 512],
            start=True, stop=True),
            reads=[bwqk_r, bcq_r], writes=[bP[2 + h]])
        fw.emit("scalar", lambda e, h=h: e.activation(
            q_sb[:, h * 512:(h + 1) * 512], P[2 + h][:], AF.Identity,
            scale=cv[:, 4:5], bias=bqkh[:]),
            reads=[bP[2 + h], bbqkh, bcv], writes=[bq], skip_self_wait=True)
    fw.emit("scalar", lambda e: e.activation(qabs[:], q_sb[:], AF.Abs),
            reads=[bq], writes=[bqabs])
    fw.emit("vector", lambda e: e.tensor_scalar_mul(qabs[:], qabs[:], mcol[:]),
            reads=[bqabs, bmcol], writes=[bqabs])
    for h in range(2):
        fw.emit("tensor", lambda e, h=h: e.matmul(
            P[2 + h][0:32, :], e4_sb[:],
            qabs[:, h * 512:(h + 1) * 512], start=True, stop=True),
            reads=[be4, bqabs], writes=[bP[2 + h]])
        fw.emit("scalar", lambda e, h=h: e.activation(
            crow[0:32, h * 512:(h + 1) * 512], P[2 + h][0:32, :], AF.Identity,
            scale=cv[0:32, 3:4], bias=cv[0:32, 5:6]),
            reads=[bP[2 + h], bcv], writes=[bcrow], skip_self_wait=True)
    for ch in range(64):
        hh = ch // 2
        half = ch % 2
        asl = ch % 2
        q0 = half * 512
        fw.dma("qscal", lambda e, hh=hh, q0=q0, asl=asl: e.dma_start(
            qasm_f[asl][0:4, :], q_sb[hh * 4:(hh + 1) * 4, q0:q0 + 512]),
            reads=[bq], writes=[bqasm_f[asl]], group="qasmA")
        fw.dma("qscal", lambda e, hh=hh, q0=q0, asl=asl: e.dma_start(
            qasm_f[asl][4:5, :], crow[hh:hh + 1, q0:q0 + 512]),
            reads=[bcrow], writes=[bqasm_f[asl]], group="qasmB")
        fw.emit("vector", lambda e, asl=asl: e.tensor_copy(
            qasm[asl][0:5, :], qasm_f[asl][0:5, :]),
            reads=[bqasm_f[asl]], writes=[bqasm[asl]])
        for kt in range(32):
            spc = kt % 2
            fw.emit("tensor", lambda e, kt=kt, asl=asl, spc=spc: e.matmul(
                P[spc][:], ckv_r[:, kt * 128:(kt + 1) * 128], qasm[asl][:],
                start=True, stop=True),
                reads=[bckv_r, bqasm[asl]], writes=[bP[spc]])
            fw.emit("scalar", lambda e, kt=kt, spc=spc: e.activation(
                P_sb[spc][:], P[spc][:], AF.Exp),
                reads=[bP[spc]], writes=[bP_sb[spc]])
            fw.emit("tensor", lambda e, kt=kt, spc=spc: e.matmul(
                P[4][0:5, :], ckvT[:, kt * 5:(kt + 1) * 5], P_sb[spc][:],
                start=(kt == 0), stop=(kt == 31)),
                reads=[bckvT, bP_sb[spc]], writes=[bP[4]], skip_self_wait=True)
        fw.emit("scalar", lambda e: e.activation(olat_h[:], P[4][0:5, :], AF.Copy),
                reads=[bP[4]], writes=[bolat])
        fw.dma("qgp", lambda e, hh=hh, q0=q0: e.dma_start(
            olat_all[hh * 4:(hh + 1) * 4, q0:q0 + 512], olat_h[0:4, :]),
            reads=[bolat], writes=[bolall], group="olat")
        fw.dma("qgp", lambda e, hh=hh, q0=q0: e.dma_start(
            den_all[hh:hh + 1, q0:q0 + 512], olat_h[4:5, :]),
            reads=[bolat], writes=[bden], group="dens")
    fw.emit("vector", lambda e: e.reciprocal(den_all[:], den_all[:]),
            reads=[bden], writes=[bden])
    for h in range(2):
        fw.emit("tensor", lambda e, h=h: e.matmul(
            P[5][:], e4t_sb[:], den_all[:, h * 512:(h + 1) * 512],
            start=True, stop=True),
            reads=[be4t, bden], writes=[bP[5]])
        fw.emit("scalar", lambda e, h=h: e.activation(
            denb[:, h * 512:(h + 1) * 512], P[5][:], AF.Copy),
            reads=[bP[5]], writes=[bdenb], skip_self_wait=True)
    fw.emit("vector", lambda e: e.tensor_tensor(
        olat_bf[:], olat_all[:], denb[:], op=ALU.mult),
        reads=[bolall, bdenb], writes=[bolbf])
    for dt32 in range(32):
        for h in range(2):
            pc = 6 + h
            fw.emit("tensor", lambda e, dt32=dt32, h=h, pc=pc: e.matmul(
                P[pc][:], wcomb_sb[:, dt32 * 128:(dt32 + 1) * 128],
                olat_bf[:, h * 512:(h + 1) * 512], start=True, stop=True),
                reads=[bwcomb, bolbf], writes=[bP[pc]])
            fw.emit("scalar", lambda e, dt32=dt32, pc=pc: e.activation(
                oev[:], P[pc][:], AF.Identity, bias=bout_sb[:, dt32:dt32 + 1]),
                reads=[bP[pc], bbout], writes=[boev2])
            fw.emit("vector", lambda e, dt32=dt32, pc=pc: e.tensor_scalar_add(
                oev2[:], oev[:], bo_sb[:, dt32:dt32 + 1]),
                reads=[boev2, bbo], writes=[boev22])
            sl2 = (dt32 * 2 + h) % 2
            fw.dma("qsync", lambda e, dt32=dt32, h=h, sl2=sl2: e.dma_start(
                xt[sl2][:, 0:512], D["x2_fm"][dt32, :, h * 512:(h + 1) * 512]),
                writes=[bxt[sl2]], group="s55")
            fw.emit("vector", lambda e, sl2=sl2: e.tensor_tensor(
                oev2[:], oev2[:], xt[sl2][:, 0:512], op=ALU.add),
                reads=[boev22, bxt[sl2]], writes=[boev22])
            fw.dma("qgp", lambda e, dt32=dt32, h=h: e.dma_start(
                D["x3_fm"][dt32, :, h * 512:(h + 1) * 512], oev2[:]),
                reads=[boev22], writes=[b_x3], group="s56")
    return dict(x3=b_x3)


def emit_l0ffn(fw, nc, D, sb, ps):
    B = fw.buf
    xt = [sb(f"gx{i}", [128, 1024], F32) for i in range(2)]
    bxt = fw.bufs(2, "gx")

    onecf = sb("gonecf", [128, 1], F32); bonecf = B()
    ones_f = sb("gof", [1, 128], F32); bof = B()
    mrow = sb("gm", [1, 1024], F32); bm = B()
    rrow = sb("gr", [1, 1024], F32); br = B()
    vrow = sb("gv", [1, 1024], F32); bv = B()
    t1 = sb("gt1", [1, 1024], F32); bt1 = B()
    t2 = sb("gt2", [1, 1024], F32); bt2 = B()
    mb = sb("gmb", [128, 1024], F32); bmb = B()
    rb = sb("grb", [128, 1024], F32); brb = B()
    cv = sb("gcv", [128, 8], F32); bcv = B()
    xtmp = sb("gxtmp", [128, 1024], F32); bxtmp = B()
    xsq = xtmp; bxsq = bxtmp
    xln = [sb(f"gxl{j}", [128, 1024], BF16) for j in range(32)]
    bxln = fw.bufs(32, "gxl")
    wq_r = sb("gwqr", [128, 32 * 64], BF16); bwq_r = B()

    q65r = sb("gq65r", [65, 1024], BF16); bq65r = B()
    wr_sb = sb("gwr", [64, 48], F32); bwr = B()
    wr_r = sb("gwrr", [64, 48], BF16); bwr_r = B()
    ak_sb = sb("gak", [64, 1024], F32); bak = B()
    ak_r = sb("gakr", [64, 1024], BF16); bak_r = B()
    aks65 = sb("gaks", [65, 1024], BF16); baks = B()
    aks_f = sb("gaksf", [65, 1024], F32); baks_f = B()
    q65 = aks_f; bq65 = baks_f
    rk49 = sb("grk49", [49, 1024], BF16); brk49 = B()
    rk_f = sb("grkf", [49, 1024], F32); brk_f = B()
    rq49 = sb("grq49", [49, 1024], BF16); brq49 = B()
    rq_f = rk_f; brq_f = brk_f
    sr = sb("gsr", [128, 1024], F32); bsr = B()
    gateb = sr; bgateb = bsr
    sw = sb("gsw", [128, 1024], F32); bsw = B()
    mcolx = sb("gmcol", [128, 1], F32); bmcolx = B()
    maskt = sb("gmask", [128, 1024], BF16); bmask = B()
    s2t = sb("gs2", [128, 1024], F32); bs2 = B()
    auxev = s2t; bauxev = bs2
    expt = sb("gexp", [128, 1024], F32); bexp = B()
    dcol = sb("gd", [128, 1], F32); bd = B()
    wbf = sb("gwbf", [128, 1024], BF16); bwbf = B()
    wT_all = sb("gwT", [128, 8 * 1024], BF16); bwT = B()
    avt_sb = [sb(f"gavt{i}", [128, 1024], BF16) for i in range(2)]
    bavt = fw.bufs(2, "gavt")
    wvd_sb = [sb(f"gwvd{i}", [128, 256], BF16) for i in range(2)]
    bwvd = fw.bufs(2, "gwvd")
    avd = sb("gavd", [128, 8 * 256], BF16); bavd = B()
    wvu_sb = [sb(f"gwvu{i}", [128, 256], BF16) for i in range(2)]
    bwvu = fw.bufs(2, "gwvu")
    wu1_sb = [sb(f"gwu1{i}", [128, 32 * 128], BF16) for i in range(2)]
    bwu1 = fw.bufs(2, "gwu1")
    wu2_sb = sb("gwu2", [128, 8], F32); bwu2 = B()
    wu2_b = sb("gwu2b", [128, 8], BF16); bwu2_b = B()
    bu1_sb = sb("gbu1", [128, 8], F32); bbu1 = B()
    bu2_sb = sb("gbu2", [1, 1], F32); bbu2 = B()
    gw1_sb = sb("ggw1", [1, 1], F32); bgw1 = B()
    gb_sb = sb("ggb", [1, 1], F32); bgb = B()
    _g1 = sb("gg1_0", [128, 1024], BF16)
    g1 = [_g1, _g1]
    _bg1 = B("gg1")
    bg1 = [_bg1, _bg1]
    lrn = sb("glrn", [1, 1024], F32); blrn = B()
    gate = lrn; bgate = blrn
    urow = sb("gu", [1, 1024], F32); bu = B()

    lvp = sb("glvp", [1, 8], F32); blvp = B()

    alat = sb("galat", [128, 2 * 1024], BF16); balat = B()

    P = [ps(f"gp{i}", [128, 512], F32) for i in range(7)]
    bP = fw.bufs(7, "gpb")
    P = P + [P[6]]
    bP = bP + [bP[6]]
    Pb = ps("gpbf", [128, 128], BF16); bPb = B()
    idn_b = sb("gidnb", [128, 128], BF16); bidn_b = B()
    b_lvd = B(); b_lvfull = B(); b_x2 = B()

    fw.dma("qsync", lambda e: e.dma_start(cv[:], D["constv"]), writes=[bcv], group="s57")
    fw.dma("qsync", lambda e: e.dma_start(ones_f[:], D["onesv"]), writes=[bof], group="p0row")
    fw.dma("qsync", lambda e: e.dma_start(onecf[:], D["onescol"]), writes=[bonecf], group="s58")
    fw.dma("qsync", lambda e: e.dma_start(xtmp[:, 0:128], D["idn"]), writes=[bxtmp], group="s59")
    fw.emit("vector", lambda e: e.tensor_copy(idn_b[:], xtmp[:, 0:128]),
            reads=[bxtmp], writes=[bidn_b])
    fw.dma("qsync", lambda e: e.dma_start(
        xt[0][:].rearrange("p (k f) -> p k f", k=16),
        D["wqT"][0:2048, :].rearrange("(k p) f -> p k f", p=128)), writes=[bxt[0]], group="s60")
    fw.dma("qsync", lambda e: e.dma_start(
        xt[1][:].rearrange("p (k f) -> p k f", k=16),
        D["wqT"][2048:4096, :].rearrange("(k p) f -> p k f", p=128)), writes=[bxt[1]], group="s61")
    fw.dma("qsync", lambda e: e.dma_start(wr_sb[:], D["wrT"]), writes=[bwr], group="p64ld")
    fw.dma("qsync", lambda e: e.dma_start(ak_sb[:], D["akT"]), writes=[bak], group="p64ld")
    fw.dma("qsync", lambda e: e.dma_start(bu1_sb[:], D["bu1_t"]), writes=[bbu1], group="s62")
    fw.dma("qsync", lambda e: e.dma_start(bu2_sb[:], D["bu2"]), writes=[bbu2], group="p0row")
    fw.dma("qsync", lambda e: e.dma_start(gw1_sb[:], D["gw1"]), writes=[bgw1], group="p0row")
    fw.dma("qsync", lambda e: e.dma_start(gb_sb[:], D["gb"]), writes=[bgb], group="p0row")
    fw.dma("qsync", lambda e: e.dma_start(
        wu2_sb[:].rearrange("p (k f) -> p k f", k=8),
        D["wu2T"].rearrange("(k p) f -> p k f", p=128),
        allow_slow_non_contiguous=True), writes=[bwu2], group="s63")
    fw.emit("vector", lambda e: e.tensor_copy(wq_r[:, 0:1024], xt[0][:]),
            reads=[bxt[0]], writes=[bwq_r])
    fw.emit("vector", lambda e: e.tensor_copy(wq_r[:, 1024:2048], xt[1][:]),
            reads=[bxt[1]], writes=[bwq_r])
    fw.emit("vector", lambda e: e.tensor_copy(wr_r[:], wr_sb[:]),
            reads=[bwr], writes=[bwr_r])
    fw.emit("vector", lambda e: e.tensor_copy(ak_r[:], ak_sb[:]),
            reads=[bak], writes=[bak_r])
    fw.emit("vector", lambda e: e.tensor_copy(wu2_b[:], wu2_sb[:]),
            reads=[bwu2], writes=[bwu2_b])
    fw.emit("vector", lambda e: e.tensor_scalar_mul(aks_f[0:64, :], ak_sb[:], 0.125),
            reads=[bak], writes=[baks_f])
    fw.dma("qscal", lambda e: e.dma_start(aks_f[64:65, :], D["log_rel"]),
           reads=[baks_f], writes=[baks_f], group="p64b")
    fw.emit("vector", lambda e: e.tensor_copy(aks65[:], aks_f[:]),
            reads=[baks_f], writes=[baks])

    for j in range(32):
        sl = j % 2
        fw.dma("qsync", lambda e, j=j, sl=sl: e.dma_start(
            xt[sl][:], D["x1_fm"][j]), writes=[bxt[sl]], group="s64")
        fw.emit("scalar", lambda e, sl=sl: e.activation(xsq[:], xt[sl][:], AF.Square),
                reads=[bxt[sl]], writes=[bxsq])
        for h in range(2):
            fw.emit("tensor", lambda e, j=j, h=h, sl=sl: e.matmul(
                P[4 + h][0:1, :], onecf[:], xt[sl][:, h * 512:(h + 1) * 512],
                start=(j == 0), stop=(j == 31)),
                reads=[bxt[sl], bonecf], writes=[bP[4 + h]], skip_self_wait=True)
            fw.emit("tensor", lambda e, j=j, h=h: e.matmul(
                P[4 + h][32:33, :], onecf[:], xsq[:, h * 512:(h + 1) * 512],
                start=(j == 0), stop=(j == 31)),
                reads=[bxsq, bonecf], writes=[bP[4 + h]], skip_self_wait=True)
    for h in range(2):
        fw.emit("vector", lambda e, h=h: e.tensor_scalar_mul(
            mrow[0:1, h * 512:(h + 1) * 512], P[4 + h][0:1, :], 1.0 / 4096.0),
            reads=[bP[4 + h]], writes=[bm], skip_self_wait=True)
        fw.emit("vector", lambda e, h=h: e.tensor_scalar_mul(
            t1[0:1, h * 512:(h + 1) * 512], P[4 + h][32:33, :], 1.0 / 4096.0),
            reads=[bP[4 + h]], writes=[bt1], skip_self_wait=True)
    fw.emit("scalar", lambda e: e.activation(rrow[:], mrow[:], AF.Square),
            reads=[bm], writes=[br])
    fw.emit("vector", lambda e: e.tensor_tensor(vrow[:], t1[:], rrow[:], op=ALU.subtract),
            reads=[bt1, br], writes=[bv])
    fw.emit("scalar", lambda e: e.activation(t1[:], vrow[:], AF.Ln, bias=cv[0:1, 0:1]),
            reads=[bv, bcv], writes=[bt1])
    fw.emit("scalar", lambda e: e.activation(rrow[:], t1[:], AF.Exp, scale=cv[0:1, 2:3]),
            reads=[bt1], writes=[br])
    fw.emit("scalar", lambda e: e.activation(t2[:], rrow[:], AF.Square),
            reads=[br], writes=[bt2])
    fw.emit("vector", lambda e: e.tensor_tensor(t2[:], t2[:], vrow[:], op=ALU.mult),
            reads=[bt2, bv], writes=[bt2])
    fw.emit("scalar", lambda e: e.activation(
        lrn[:], t2[:], AF.Ln, bias=cv[0:1, 1:2], accum_out=lvp[0:1, 0:1]),
        reads=[bt2, bcv], writes=[blrn, blvp])
    fw.dma("qgp", lambda e: e.dma_start(D["lv_part"], lvp[:]),
           reads=[blvp], writes=[b_lvd], group="p0st")
    fw.collective(lambda e: e.collective_compute(
        "AllReduce", ALU.add, [list(range(8))], [D["lv_part"]], [D["lv_full"]]),
        reads=[b_lvd], writes=[b_lvfull])
    fw.dma("qsync", lambda e: e.dma_start(lvp[:], D["lv_full"]),
           reads=[b_lvfull], writes=[blvp], group="p0row")
    for h in range(2):
        fw.emit("tensor", lambda e, h=h: e.matmul(
            P[4][:], ones_f[:], mrow[0:1, h * 512:(h + 1) * 512], start=True, stop=True),
            reads=[bof, bm], writes=[bP[4]])
        fw.emit("scalar", lambda e, h=h: e.activation(
            mb[:, h * 512:(h + 1) * 512], P[4][:], AF.Copy),
            reads=[bP[4]], writes=[bmb], skip_self_wait=True)
        fw.emit("tensor", lambda e, h=h: e.matmul(
            P[4][:], ones_f[:], rrow[0:1, h * 512:(h + 1) * 512], start=True, stop=True),
            reads=[bof, br], writes=[bP[4]])
        fw.emit("scalar", lambda e, h=h: e.activation(
            rb[:, h * 512:(h + 1) * 512], P[4][:], AF.Copy),
            reads=[bP[4]], writes=[brb], skip_self_wait=True)
    for j in range(32):
        sl = j % 2
        fw.dma("qsync", lambda e, j=j, sl=sl: e.dma_start(
            xt[sl][:], D["x1_fm"][j]), writes=[bxt[sl]], group="s65")
        fw.emit("vector", lambda e, sl=sl: e.tensor_tensor(
            xtmp[:], xt[sl][:], mb[:], op=ALU.subtract),
            reads=[bxt[sl], bmb], writes=[bxtmp])
        fw.emit("vector", lambda e, j=j: e.tensor_tensor(
            xln[j][:], xtmp[:], rb[:], op=ALU.mult),
            reads=[bxtmp, brb], writes=[bxln[j]])
        for h in range(2):
            fw.emit("tensor", lambda e, j=j, h=h: e.matmul(
                P[h][0:64, :], wq_r[:, j * 64:(j + 1) * 64],
                xln[j][:, h * 512:(h + 1) * 512],
                start=(j == 0), stop=(j == 31)),
                reads=[bwq_r, bxln[j]], writes=[bP[h]], skip_self_wait=True)
    for h in range(2):
        fw.emit("scalar", lambda e, h=h: e.activation(
            q65[0:64, h * 512:(h + 1) * 512], P[h][0:64, :], AF.Copy),
            reads=[bP[h]], writes=[bq65], skip_self_wait=True)
    fw.dma("qscal", lambda e: e.dma_start(q65[64:65, :], D["ones4k"][0:1, 0:1024]),
           reads=[bq65], writes=[bq65], group="p64b")
    fw.emit("vector", lambda e: e.tensor_copy(q65r[:], q65[:]),
            reads=[bq65], writes=[bq65r])
    fw.emit("tensor", lambda e: e.matmul(
        P[2][0:48, :], wr_r[:], ak_r[:, 0:512], start=True, stop=True),
        reads=[bwr_r, bak_r], writes=[bP[2]])
    fw.emit("tensor", lambda e: e.matmul(
        P[3][0:48, :], wr_r[:], ak_r[:, 512:1024], start=True, stop=True),
        reads=[bwr_r, bak_r], writes=[bP[3]])
    fw.emit("scalar", lambda e: e.activation(rk_f[0:48, 0:512], P[2][0:48, :], AF.Copy),
            reads=[bP[2]], writes=[brk_f])
    fw.emit("scalar", lambda e: e.activation(rk_f[0:48, 512:1024], P[3][0:48, :], AF.Copy),
            reads=[bP[3]], writes=[brk_f], skip_self_wait=True)
    fw.dma("qscal", lambda e: e.dma_start(rk_f[48:49, :], D["log_rel"]),
           reads=[brk_f], writes=[brk_f], group="p48b")
    fw.emit("vector", lambda e: e.tensor_copy(rk49[:], rk_f[:]),
            reads=[brk_f], writes=[brk49])
    for h in range(2):
        fw.emit("tensor", lambda e, h=h: e.matmul(
            P[2][0:48, :], wr_r[:], q65r[0:64, h * 512:(h + 1) * 512],
            start=True, stop=True),
            reads=[bwr_r, bq65r], writes=[bP[2]])
        fw.emit("scalar", lambda e, h=h: e.activation(
            rq_f[0:48, h * 512:(h + 1) * 512], P[2][0:48, :], AF.Identity,
            scale=cv[0:48, 6:7]),
            reads=[bP[2], bcv], writes=[brq_f], skip_self_wait=True)
    fw.dma("qscal", lambda e: e.dma_start(rq_f[48:49, :], D["ones4k"][0:1, 0:1024]),
           reads=[brq_f], writes=[brq_f], group="p48b")
    fw.emit("vector", lambda e: e.tensor_copy(rq49[:], rq_f[:]),
            reads=[brq_f], writes=[brq49])
    for j in range(32):
        sl = j % 2
        fw.dma("qgp", lambda e, j=j, sl=sl: e.dma_start(
            avt_sb[sl][:], D["avT"][j * 128:(j + 1) * 128, :]), writes=[bavt[sl]], group="s66")
        fw.dma("qgp", lambda e, j=j, sl=sl: e.dma_start(
            wvd_sb[sl][:], D["wvdT"][j * 128:(j + 1) * 128, :]), writes=[bwvd[sl]], group="s67")
        def mm_avd(e, j=j, sl=sl):
            last = None
            for st in range(8):
                last = e.matmul(P[st // 2][:, (st % 2) * 256:(st % 2) * 256 + 256],
                                avt_sb[sl][:, st * 128:(st + 1) * 128],
                                wvd_sb[sl][:], start=(j == 0), stop=(j == 31))
            return last
        fw.emit("tensor", mm_avd, reads=[bavt[sl], bwvd[sl]],
                writes=[bP[0], bP[1], bP[2], bP[3]], skip_self_wait=True)
    for st in range(8):
        fw.emit("scalar", lambda e, st=st: e.activation(
            avd[:, st * 256:(st + 1) * 256],
            P[st // 2][:, (st % 2) * 256:(st % 2) * 256 + 256], AF.Copy),
            reads=[bP[st // 2]], writes=[bavd], skip_self_wait=True)
    for tt in range(8):
        t0 = tt * 128
        for h in range(2):
            fw.emit("tensor", lambda e, tt=tt, h=h, t0=t0: e.matmul(
                P[h][:], rq49[:, t0:t0 + 128], rk49[:, h * 512:(h + 1) * 512],
                start=True, stop=True),
                reads=[brq49, brk49], writes=[bP[h]])
            fw.emit("scalar", lambda e, h=h: e.activation(
                sr[:, h * 512:(h + 1) * 512], P[h][:], AF.Copy),
                reads=[bP[h]], writes=[bsr], skip_self_wait=True)
        fw.emit("vector", lambda e: e.tensor_copy(sw[:], sr[:]),
                reads=[bsr], writes=[bsw])
        for it8 in range(8):
            fw.emit("vector", lambda e: e.reduce_max(mcolx[:], sw[:], axis=AX.X),
                    reads=[bsw], writes=[bmcolx])
            if it8 < 7:
                fw.emit("vector", lambda e: e.tensor_scalar(
                    maskt[:], sw[:], mcolx[:], None, op0=ALU.is_ge),
                    reads=[bsw, bmcolx], writes=[bmask])
                fw.emit("vector", lambda e: e.scalar_tensor_tensor(
                    sw[:], maskt[:], -1e30, sw[:], op0=ALU.mult, op1=ALU.add),
                    reads=[bmask, bsw], writes=[bsw])
        fw.emit("vector", lambda e: e.tensor_scalar(
            maskt[:], sr[:], mcolx[:], None, op0=ALU.is_ge),
            reads=[bsr, bmcolx], writes=[bmask])
        for h in range(2):
            fw.emit("tensor", lambda e, tt=tt, h=h, t0=t0: e.matmul(
                P[h][:], q65r[:, t0:t0 + 128], aks65[:, h * 512:(h + 1) * 512],
                start=True, stop=True),
                reads=[bq65r, baks], writes=[bP[h]])
            fw.emit("scalar", lambda e, h=h: e.activation(
                s2t[:, h * 512:(h + 1) * 512], P[h][:], AF.Exp),
                reads=[bP[h]], writes=[bs2], skip_self_wait=True)
        fw.emit("vector", lambda e: e.tensor_tensor(expt[:], s2t[:], maskt[:], op=ALU.mult),
                reads=[bs2, bmask], writes=[bexp])
        fw.emit("vector", lambda e: e.reduce_sum(dcol[:], expt[:], axis=AX.X),
                reads=[bexp], writes=[bd])
        fw.emit("vector", lambda e: e.reciprocal(dcol[:], dcol[:]),
                reads=[bd], writes=[bd])
        fw.emit("vector", lambda e: e.tensor_scalar(
            wbf[:], expt[:], dcol[:], None, op0=ALU.mult),
            reads=[bexp, bd], writes=[bwbf])
        for st in range(8):
            fw.emit("tensor", lambda e, st=st: e.transpose(
                Pb[:], wbf[:, st * 128:(st + 1) * 128], idn_b[:]),
                reads=[bwbf, bidn_b], writes=[bPb])
            fw.emit("scalar", lambda e, st=st, t0=t0: e.activation(
                wT_all[:, st * 1024 + t0:st * 1024 + t0 + 128], Pb[:], AF.Copy),
                reads=[bPb], writes=[bwT], skip_self_wait=True)
    for mhalf in range(2):
        for h in range(2):
            def mm_alat(e, mhalf=mhalf, h=h):
                last = None
                for st in range(8):
                    last = e.matmul(
                        P[2 + mhalf * 2 + h][:], avd[:, st * 256 + mhalf * 128:st * 256 + (mhalf + 1) * 128],
                        wT_all[:, st * 1024 + h * 512:st * 1024 + (h + 1) * 512],
                        start=(st == 0), stop=(st == 7))
                return last
            fw.emit("tensor", mm_alat, reads=[bavd, bwT], writes=[bP[2 + mhalf * 2 + h]])
            fw.emit("scalar", lambda e, mhalf=mhalf, h=h: e.activation(
                alat[:, mhalf * 1024 + h * 512:mhalf * 1024 + (h + 1) * 512],
                P[2 + mhalf * 2 + h][:], AF.Copy),
                reads=[bP[2 + mhalf * 2 + h]], writes=[balat], skip_self_wait=True)
    for mt in range(8):
        msl = mt % 2
        fw.dma("qgp", lambda e, mt=mt, msl=msl: e.dma_start(
            wu1_sb[msl][:].rearrange("p (k f) -> p k f", k=32),
            D["wu1T"][:, mt * 128:(mt + 1) * 128].rearrange("(k p) f -> p k f", p=128)),
            writes=[bwu1[msl]], group="s68")
        for h in range(2):
            def mm_u1(e, mt=mt, h=h, msl=msl):
                last = None
                for j in range(32):
                    last = e.matmul(P[4 + h][:], wu1_sb[msl][:, j * 128:(j + 1) * 128],
                                    xln[j][:, h * 512:(h + 1) * 512],
                                    start=(j == 0), stop=(j == 31))
                return last
            fw.emit("tensor", mm_u1, reads=[bwu1[msl]] + bxln, writes=[bP[4 + h]])
            fw.emit("scalar", lambda e, mt=mt, h=h, msl=msl: e.activation(
                g1[msl][:, h * 512:(h + 1) * 512], P[4 + h][:],
                AF.Gelu, bias=bu1_sb[:, mt:mt + 1]),
                reads=[bP[4 + h], bbu1], writes=[bg1[msl]], skip_self_wait=True)
        for h in range(2):
            fw.emit("tensor", lambda e, mt=mt, h=h, msl=msl: e.matmul(
                P[6][32 * h:32 * h + 1, :], wu2_b[:, mt:mt + 1],
                g1[msl][:, h * 512:(h + 1) * 512],
                start=(mt == 0), stop=(mt == 7)),
                reads=[bwu2_b, bg1[msl]], writes=[bP[6]], skip_self_wait=True)
    for h in range(2):
        fw.emit("scalar", lambda e, h=h: e.activation(
            t2[0:1, h * 512:(h + 1) * 512], P[6][32 * h:32 * h + 1, :], AF.Identity,
            bias=bu2_sb[:]),
            reads=[bP[6], bbu2], writes=[bt2], skip_self_wait=True)
    fw.emit("scalar", lambda e: e.activation(urow[:], t2[:], AF.Sigmoid),
            reads=[bt2], writes=[bu])
    fw.emit("vector", lambda e: e.tensor_scalar(
        t2[0:1, 0:1], lvp[0:1, 0:1], 1.0 / 8192.0, 1e-6, op0=ALU.mult, op1=ALU.add),
        reads=[blvp], writes=[bt2])
    fw.emit("vector", lambda e: e.reciprocal(t2[0:1, 0:1], t2[0:1, 0:1]),
            reads=[bt2], writes=[bt2])
    fw.emit("vector", lambda e: e.tensor_scalar(
        urow[:], urow[:], 2.5, None, op0=ALU.mult),
        reads=[bu], writes=[bu])
    fw.emit("vector", lambda e: e.scalar_tensor_tensor(
        t1[:], lrn[:], t2[0:1, 0:1], urow[:], op0=ALU.mult, op1=ALU.bypass),
        reads=[blrn, bt2, bu], writes=[bt1]) if False else None
    fw.emit("vector", lambda e: e.tensor_scalar(
        t1[:], lrn[:], t2[0:1, 0:1], 0.5, op0=ALU.mult, op1=ALU.mult),
        reads=[blrn, bt2], writes=[bt1])
    fw.emit("vector", lambda e: e.tensor_tensor(urow[:], urow[:], t1[:], op=ALU.add),
            reads=[bu, bt1], writes=[bu])
    fw.emit("vector", lambda e: e.tensor_scalar(
        urow[:], urow[:], 0.0, 5.0, op0=ALU.max, op1=ALU.min),
        reads=[bu], writes=[bu])
    fw.emit("vector", lambda e: e.tensor_scalar(
        urow[:], urow[:], -0.5, 1.0 / 1.5, op0=ALU.add, op1=ALU.mult),
        reads=[bu], writes=[bu])
    fw.emit("vector", lambda e: e.tensor_scalar(
        urow[:], urow[:], 0.0, 1.0, op0=ALU.max, op1=ALU.min),
        reads=[bu], writes=[bu])
    fw.emit("vector", lambda e: e.tensor_scalar(
        urow[:], urow[:], gw1_sb[0:1, 0:1], gb_sb[0:1, 0:1], op0=ALU.mult, op1=ALU.add),
        reads=[bu, bgw1, bgb], writes=[bu])
    fw.emit("scalar", lambda e: e.activation(gate[:], urow[:], AF.Sigmoid),
            reads=[bu], writes=[bgate])
    fw.emit("vector", lambda e: e.tensor_scalar(t1[:], gate[:], 0.05, None, op0=ALU.is_ge),
            reads=[bgate], writes=[bt1])
    fw.emit("vector", lambda e: e.tensor_tensor(gate[:], gate[:], t1[:], op=ALU.mult),
            reads=[bgate, bt1], writes=[bgate])
    for h in range(2):
        fw.emit("tensor", lambda e, h=h: e.matmul(
            P[6][:], ones_f[:], gate[0:1, h * 512:(h + 1) * 512], start=True, stop=True),
            reads=[bof, bgate], writes=[bP[6]])
        fw.emit("scalar", lambda e, h=h: e.activation(
            gateb[:, h * 512:(h + 1) * 512], P[6][:], AF.Copy),
            reads=[bP[6]], writes=[bgateb], skip_self_wait=True)
    for j in range(32):
        sl = j % 2
        fw.dma("qgp", lambda e, j=j, sl=sl: e.dma_start(
            wvu_sb[sl][:].rearrange("p (k f) -> p k f", k=2),
            D["wvuT2"][:, j * 128:(j + 1) * 128].rearrange("(k p) f -> p k f", p=128)),
            writes=[bwvu[sl]], group="s69")
        for h in range(2):
            def mm_aux(e, j=j, h=h, sl=sl):
                last = None
                for kt in range(2):
                    last = e.matmul(P[h][:], wvu_sb[sl][:, kt * 128:(kt + 1) * 128],
                                    alat[:, kt * 1024 + h * 512:kt * 1024 + (h + 1) * 512],
                                    start=(kt == 0), stop=(kt == 1))
                return last
            fw.emit("tensor", mm_aux, reads=[bwvu[sl], balat], writes=[bP[h]])
        fw.dma("qsync", lambda e, j=j, sl=sl: e.dma_start(
            xt[sl][:], D["x1_fm"][j]), writes=[bxt[sl]], group="s70")
        for h in range(2):
            fw.emit("vector", lambda e, h=h: e.tensor_tensor(
                auxev[:, h * 512:(h + 1) * 512], P[h][:], gateb[:, h * 512:(h + 1) * 512],
                op=ALU.mult),
                reads=[bP[h], bgateb], writes=[bauxev], skip_self_wait=True)
        fw.emit("vector", lambda e, sl=sl: e.tensor_tensor(
            xtmp[:], xt[sl][:], mb[:], op=ALU.subtract),
            reads=[bxt[sl], bmb], writes=[bxtmp])
        fw.emit("vector", lambda e: e.tensor_tensor(
            xtmp[:], xtmp[:], rb[:], op=ALU.mult),
            reads=[bxtmp, brb], writes=[bxtmp])
        fw.emit("vector", lambda e, sl=sl: e.tensor_tensor(
            xtmp[:], xtmp[:], xt[sl][:], op=ALU.add),
            reads=[bxtmp, bxt[sl]], writes=[bxtmp])
        fw.emit("vector", lambda e: e.tensor_tensor(
            xtmp[:], xtmp[:], auxev[:], op=ALU.add),
            reads=[bxtmp, bauxev], writes=[bxtmp])
        fw.dma("qgp", lambda e, j=j: e.dma_start(D["x2_fm"][j], xtmp[:]),
               reads=[bxtmp], writes=[b_x2], group="s71")
    return dict(x2=b_x2)


import numpy as np
from contextlib import ExitStack
import concourse.bass as bass
from concourse.bass_utils import run_bass_kernel_spmd


def _mkdecl(nc):
    D = {}
    def ext_in(name, shape, dt=F32):
        D[name] = nc.dram_tensor(name, shape, dt, kind="ExternalInput").ap()
    def ext_out(name, shape, dt=F32):
        D[name] = nc.dram_tensor(name, shape, dt, kind="ExternalOutput").ap()
    def internal(name, shape, dt=F32):
        D[name] = nc.dram_tensor(name, shape, dt, kind="Internal").ap()
    return D, ext_in, ext_out, internal


def _finish(nc, stack, fw, emitfn):
    def sb(name, shape, dt=F32):
        return stack.enter_context(nc.sbuf_tensor(name, shape, dt))
    def ps(name, shape, dt=F32):
        return stack.enter_context(nc.psum_tensor(name, shape, dt))
    emitfn(sb, ps)
    block = stack.enter_context(nc.Block())
    fw.finalize(block)


def build_mamba():
    nc = bass.Bass(target_bir_lowering=False)
    D, ext_in, ext_out, internal = _mkdecl(nc)
    ext_in("x_ds", [2, 4096, 512])
    ext_in("xprojT", [512, 160])
    ext_in("dtwT", [128, 512])
    ext_in("dtb", [128, 4])
    ext_in("alog", [512, 16])
    ext_in("dvec", [128, 4])
    ext_in("idn", [128, 128])
    ext_in("constv", [128, 8])
    ext_in("onesv", [1, 128])
    internal("stats_part", [128, 128])
    internal("stats_full", [128, 128])
    internal("xp_part", [160, 8192])
    internal("xp_full", [160, 8192])
    internal("xT_dram", [4, 128, 8192])
    internal("xlnT_dram", [4, 128, 8192])
    internal("dt_dram", [4, 128, 8192])
    internal("bc2", [64, 8192], BF16)
    ext_out("x1_out", [512, 8192])
    with ExitStack() as stack:
        fw = FW(nc)
        _finish(nc, stack, fw, lambda sb, ps: emit_mamba(fw, nc, D, sb, ps))
    return nc


def build_l0ffn():
    nc = bass.Bass(target_bir_lowering=False)
    D, ext_in, ext_out, internal = _mkdecl(nc)
    ext_in("x1_fm", [32, 128, 1024])
    ext_in("constv", [128, 8])
    ext_in("onesv", [1, 128])
    ext_in("onescol", [128, 1])
    ext_in("ones4k", [1, 4096])
    ext_in("idn", [128, 128])
    ext_in("wqT", [4096, 64])
    ext_in("wrT", [64, 48])
    ext_in("akT", [64, 1024])
    ext_in("avT", [4096, 1024])
    ext_in("wvdT", [4096, 256])
    ext_in("wvuT2", [256, 4096])
    ext_in("wu1T", [4096, 1024])
    ext_in("wu2T", [1024, 1])
    ext_in("bu1_t", [128, 8])
    ext_in("bu2", [1, 1])
    ext_in("gw1", [1, 1])
    ext_in("gb", [1, 1])
    ext_in("log_rel", [1, 1024])
    internal("lv_part", [1, 8])
    internal("lv_full", [1, 8])
    ext_out("x2_fm", [32, 128, 1024])
    with ExitStack() as stack:
        fw = FW(nc)
        _finish(nc, stack, fw, lambda sb, ps: emit_l0ffn(fw, nc, D, sb, ps))
    return nc


def build_attn():
    nc = bass.Bass(target_bir_lowering=False)
    D, ext_in, ext_out, internal = _mkdecl(nc)
    ext_in("x2_fm", [32, 128, 1024])
    ext_in("constv", [128, 8])
    ext_in("onesv", [1, 128])
    ext_in("onescol", [128, 1])
    ext_in("idn", [128, 128])
    ext_in("ones4k", [1, 4096])
    ext_in("wqkv", [4096, 16])
    ext_in("bqkv", [16, 1])
    ext_in("w_qk", [12, 128])
    ext_in("b_qk", [128, 1])
    ext_in("wvuT_s", [128, 4, 4])
    ext_in("wo_rows", [512, 4096])
    ext_in("bvu_s", [128, 4])
    ext_in("bo_t", [128, 32])
    ext_in("e4t", [32, 128])
    ext_in("e4", [128, 32])
    internal("ckv_part", [4, 1024])
    internal("ckv_ag", [16, 1024])
    internal("wcomb_loc", [16, 4096], BF16)
    internal("wcomb_ag", [128, 4096], BF16)
    internal("bout_part", [1, 4096])
    internal("bout_full", [1, 4096])
    internal("m4_dram", [4, 1])
    ext_out("x3_fm", [32, 128, 1024])
    with ExitStack() as stack:
        fw = FW(nc)
        _finish(nc, stack, fw, lambda sb, ps: emit_attn(fw, nc, D, sb, ps))
    return nc


def build_l1ffn():
    nc = bass.Bass(target_bir_lowering=False)
    D, ext_in, ext_out, internal = _mkdecl(nc)
    ext_in("x3_fm", [32, 128, 1024])
    ext_in("constv", [128, 8])
    ext_in("onesv", [1, 128])
    ext_in("onescol", [128, 1])
    ext_in("sg_t", [128, 86])
    ext_in("su_t", [128, 86])
    ext_in("sd_t", [128, 32])
    ext_in("wgT", [4096, 11008], I32)
    ext_in("wuT", [4096, 11008], I32)
    ext_in("wdT", [11008, 4096], I32)
    internal("hact_dram", [2, 86, 128, 512], BF16)
    ext_out("out_fm", [32, 128, 1024])
    with ExitStack() as stack:
        fw = FW(nc)
        _finish(nc, stack, fw, lambda sb, ps: emit_l1ffn(fw, nc, D, sb, ps))
    return nc


CONSTV = np.tile(np.array([[1e-5, 1.0, -0.5, -1.0, 0.5, 50.0,
                            1.0 / np.sqrt(48.0), 0.125]], np.float32), (128, 1))
ONESV = np.ones((1, 128), np.float32)
ONESCOL = np.ones((128, 1), np.float32)
ONES4K = np.ones((1, 4096), np.float32)
IDN = np.eye(128, dtype=np.float32)
E4T = np.zeros((32, 128), np.float32)
for _h in range(32):
    E4T[_h, 4 * _h:4 * _h + 4] = 1.0
E4 = np.ascontiguousarray(E4T.T)


def kernel(**ins):
    C = np.ascontiguousarray
    x = ins["L0_attn_x"]
    xproj = ins["L0_attn_x_proj_w"]
    dtw = ins["L0_attn_dt_proj_w"]
    dtb = ins["L0_attn_dt_proj_b"]
    alog = ins["L0_attn_A_log"]
    dvec = ins["L0_attn_D"]
    maps = []
    for c in range(8):
        sl = slice(512 * c, 512 * (c + 1))
        maps.append(dict(
            x_ds=C(x[:, :, sl]), xprojT=C(xproj[:, sl].T), dtwT=C(dtw[sl, :].T),
            dtb=C(dtb[sl].reshape(4, 128).T), alog=C(alog[sl, :]),
            dvec=C(dvec[sl].reshape(4, 128).T), idn=IDN, constv=CONSTV, onesv=ONESV))
    res = run_bass_kernel_spmd(build_mamba(), maps, core_ids=list(range(8)))
    x1_fm = np.concatenate([res.results[c]["x1_out"] for c in range(8)], axis=0)

    z = ins
    maps = []
    for c in range(8):
        tok = slice(1024 * c, 1024 * (c + 1))
        maps.append(dict(
            x1_fm=C(x1_fm[:, tok]).reshape(32, 128, 1024),
            constv=CONSTV, onesv=ONESV, onescol=ONESCOL, ones4k=ONES4K, idn=IDN,
            wqT=C(z["L0_ffn_Wq"].T), wrT=C(z["L0_ffn_Wr"].T),
            akT=C(z["L0_ffn_aux_keys"].T), avT=C(z["L0_ffn_aux_values"].T),
            wvdT=C(z["L0_ffn_Wvd"].T), wvuT2=C(z["L0_ffn_Wvu"].T),
            wu1T=C(z["L0_ffn_Wu1"].T), wu2T=C(z["L0_ffn_Wu2"].T),
            bu1_t=C(z["L0_ffn_bu1"].reshape(8, 128).T),
            bu2=z["L0_ffn_bu2"].reshape(1, 1),
            gw1=z["L0_ffn_gate_w1"].reshape(1, 1), gb=z["L0_ffn_gate_bias"].reshape(1, 1),
            log_rel=z["L0_ffn_log_rel"].reshape(1, 1024)))
    res = run_bass_kernel_spmd(build_l0ffn(), maps, core_ids=list(range(8)))
    x2_fm = np.concatenate(
        [res.results[c]["x2_fm"].reshape(4096, 1024) for c in range(8)], axis=1)

    wqkv = np.hstack([z["L1_attn_Wq_d"], z["L1_attn_Wkv_d"]])
    bqkv = np.concatenate([z["L1_attn_bq_d"], z["L1_attn_bkv_d"]]).reshape(16, 1)
    wvuT = C(z["L1_attn_Wv_u"].T)
    maps = []
    for c in range(8):
        tok = slice(1024 * c, 1024 * (c + 1))
        hd = slice(512 * c, 512 * (c + 1))
        maps.append(dict(
            x2_fm=C(x2_fm[:, tok]).reshape(32, 128, 1024),
            constv=CONSTV, onesv=ONESV, onescol=ONESCOL, idn=IDN, ones4k=ONES4K,
            wqkv=wqkv, bqkv=bqkv, w_qk=z["L1_attn_W_qk"],
            b_qk=z["L1_attn_b_qk"].reshape(128, 1),
            wvuT_s=C(wvuT[hd].reshape(4, 128, 4).transpose(1, 0, 2)),
            wo_rows=C(z["L1_attn_Wo"][hd, :]),
            bvu_s=C(z["L1_attn_bv_u"][hd].reshape(4, 128).T),
            bo_t=C(z["L1_attn_bo"].reshape(32, 128).T),
            e4t=E4T, e4=E4))
    res = run_bass_kernel_spmd(build_attn(), maps, core_ids=list(range(8)))
    x3_fm = np.concatenate(
        [res.results[c]["x3_fm"].reshape(4096, 1024) for c in range(8)], axis=1)

    wgT = C(z["L1_ffn_w_gate"].T)
    wuT = C(z["L1_ffn_w_up"].T)
    wdT = C(z["L1_ffn_w_down"].T)
    maps = []
    for c in range(8):
        tok = slice(1024 * c, 1024 * (c + 1))
        maps.append(dict(
            x3_fm=C(x3_fm[:, tok]).reshape(32, 128, 1024),
            constv=CONSTV, onesv=ONESV, onescol=ONESCOL,
            sg_t=C(z["L1_ffn_s_gate"].reshape(86, 128).T),
            su_t=C(z["L1_ffn_s_up"].reshape(86, 128).T),
            sd_t=C(z["L1_ffn_s_down"].reshape(32, 128).T),
            wgT=wgT, wuT=wuT, wdT=wdT))
    res = run_bass_kernel_spmd(build_l1ffn(), maps, core_ids=list(range(8)))
    out_fm = np.concatenate(
        [res.results[c]["out_fm"].reshape(4096, 1024) for c in range(8)], axis=1)
    return np.ascontiguousarray(out_fm.T).reshape(2, 4096, 4096)
